# version 07
# speedup vs baseline: 1.2713x; 1.0597x over previous
import contextlib
import numpy as np
import ml_dtypes
import concourse.bass as bass
import concourse.mybir as mybir
from concourse.bass_utils import run_bass_kernel_spmd

F32 = mybir.dt.float32
BF16 = mybir.dt.bfloat16
AF = mybir.ActivationFunctionType
ALU = mybir.AluOpType
AP = bass.AP

NCORE = 8
SEQ = 2
S = 2048
E = 1024
H = 16
NST = 4
ST = 512
NCH = 8
EPS = 1e-5

TWOPI_HI = np.float32(2 * np.pi)
D_HI = np.float32(np.float64(2 * np.pi) - np.float64(TWOPI_HI))
D_LO = np.float32(np.float64(2 * np.pi) - np.float64(TWOPI_HI) - np.float64(D_HI))
PI_HI = np.float32(np.pi)
INV2PI = np.float32(1.0 / np.float64(TWOPI_HI))
R32_64 = np.float64(10000.0) ** (np.float64(1) / 32)
R32_HI = np.float32(R32_64)
R32_LO = np.float32(R32_64 - np.float64(R32_HI))

DEBUG_TAPS = []
PHASES = 99
FFN_PARTS = 99
B_PARTS = 99
B_ATT = 99
W_PARTS = 99
SIM_SAFE = False


class Pr:

    ENG = ("sync", "scalar", "vector", "tensor", "gpsimd")
    CSEM = {"scalar": "C", "vector": "V", "tensor": "T", "gpsimd": "G", "sync": "Y"}
    DMA_STREAMS = ("LX0", "LX1", "LM0", "LM1", "LL0", "LL1", "LW",
                   "SX0", "SX1", "SL0", "SL1", "SY0", "SY1", "SR",
                   "SW0", "SW1", "SW2", "SW3", "SP0", "SP1", "SO",
                   "GW0", "GW1", "GS0", "GS1")

    def __init__(self, nc, es):
        self.nc = nc
        self.ops = {e: [] for e in self.ENG}
        self.sems = {}
        self.cnt = {}
        for name in ("C", "V", "T", "G", "Y") + self.DMA_STREAMS:
            self.sems[name] = es.enter_context(nc.semaphore("sem" + name))
            self.cnt[name] = 0
        self.res = {}
        self.gps_prev = None

    def e(self, eng, fn, waits=(), reads=(), writes=(), dma=None, hard=False):
        if dma:
            sem, inc = dma, 16
        else:
            sem, inc = self.CSEM[eng], 1
        w = {}
        if eng == "gpsimd" and self.gps_prev is not None:
            w[self.gps_prev[0]] = max(w.get(self.gps_prev[0], 0), self.gps_prev[1])
        for s_, v in waits:
            if v:
                w[s_] = max(w.get(s_, 0), v)
        for r in reads:
            ent = self.res.get(r)
            if ent:
                for s_, v in ent["w"]:
                    w[s_] = max(w.get(s_, 0), v)
        for r in writes:
            ent = self.res.get(r)
            if ent:
                if ent["r"]:
                    for s_, v in ent["w"]:
                        w[s_] = max(w.get(s_, 0), v)
                    for s_, v in ent["r"]:
                        w[s_] = max(w.get(s_, 0), v)
        self.cnt[sem] += inc
        tok = self.cnt[sem]
        if eng == "gpsimd":
            self.gps_prev = (sem, tok)
        self.ops[eng].append((sorted(w.items()), fn, sem, inc, hard))
        for r in reads:
            self.res.setdefault(r, {"w": [], "r": []})["r"].append((sem, tok))
        for r in writes:
            ent = self.res.setdefault(r, {"w": [], "r": []})
            if ent["r"]:
                ent["w"] = [(sem, tok)]
                ent["r"] = []
            else:
                ent["w"].append((sem, tok))
        return tok

    def ed(self, eng, fnmaker, n, sem, waits=(), reads=(), writes=()):
        w = {}
        if eng == "gpsimd" and self.gps_prev is not None:
            w[self.gps_prev[0]] = max(w.get(self.gps_prev[0], 0), self.gps_prev[1])
        for s_, v in waits:
            if v:
                w[s_] = max(w.get(s_, 0), v)
        for r in reads:
            ent = self.res.get(r)
            if ent:
                for s_, v in ent["w"]:
                    w[s_] = max(w.get(s_, 0), v)
        for r in writes:
            ent = self.res.get(r)
            if ent and ent["r"]:
                for s_, v in ent["w"]:
                    w[s_] = max(w.get(s_, 0), v)
                for s_, v in ent["r"]:
                    w[s_] = max(w.get(s_, 0), v)
        self.cnt[sem] += 16 * n
        tok = self.cnt[sem]
        if eng == "gpsimd":
            self.gps_prev = (sem, tok)
        semh = self.sems[sem]
        def wrapped(e, fnmaker=fnmaker, semh=semh):
            for ins in fnmaker(e):
                ins.then_inc(semh, 16)
            return None
        self.ops[eng].append((sorted(w.items()), wrapped, sem, 0, False))
        for r in reads:
            self.res.setdefault(r, {"w": [], "r": []})["r"].append((sem, tok))
        for r in writes:
            ent = self.res.setdefault(r, {"w": [], "r": []})
            if ent["r"]:
                ent["w"] = [(sem, tok)]
                ent["r"] = []
            else:
                ent["w"].append((sem, tok))
        return tok

    def barrier(self):
        snap = [(s, self.cnt[s]) for s in self.cnt]
        for eng in self.ENG:
            sem = self.CSEM[eng]
            self.cnt[sem] += 1
            self.ops[eng].append((list(snap), lambda e: e.nop(), sem, 1, False))
        self.gps_prev = ("G", self.cnt["G"])
        self.res["__bar__"] = {"w": None, "r": []}

    def last(self, sem):
        return self.cnt[sem]

    def flush(self, block):
        hmap = {"sync": block.sync, "scalar": block.scalar, "vector": block.vector,
                "tensor": block.tensor, "gpsimd": block.gpsimd}
        for eng in self.ENG:
            ops = self.ops[eng]
            if not ops:
                continue

            def body(e, ops=ops, eng=eng):
                seen = {}
                for w, fn, sem, inc, hard in ops:
                    for s_, v in w:
                        if seen.get(s_, 0) < v:
                            e.wait_ge(self.sems[s_], v)
                            seen[s_] = v
                    inst = fn(e)
                    if inc:
                        inst.then_inc(self.sems[sem], inc)
            hmap[eng](body)


def build_nc(debug_taps=()):
    nc = bass.Bass("TRN2", target_bir_lowering=False)

    def din(name, shape, dt=F32):
        return nc.dram_tensor(name, shape, dt, kind="ExternalInput").ap()

    x_in = din("x", [SEQ, S, E])
    wq = din("wq", [E, E]); wk = din("wk", [E, E])
    wv = din("wv", [E, E]); wo = din("wo", [E, E])
    wqkv = din("wqkv", [E, 3 * E])
    bqkv = din("bqkv", [1, 3 * E])
    wproj = din("wproj", [E, E])
    bproj = din("bproj", [1, E])
    gam = din("gam", [1, 1])
    aexp = din("aexp", [4, 512, 512]); bexp = din("bexp", [4, 512, 512])
    aexpi = din("aexpi", [4, 512, 512]); bexpi = din("bexpi", [4, 512, 512])
    bffn = din("bffn", [1, E])

    out_t = nc.dram_tensor("out", [SEQ, S, E], F32, kind="ExternalOutput").ap()
    xscr = nc.dram_tensor("xscr", [SEQ, NST, 128, NCH * ST], F32, kind="Internal").ap()
    lnscr = nc.dram_tensor("lnscr", [NST, 128, NCH * ST], BF16, kind="Internal").ap()
    wscr = nc.dram_tensor("wscr", [4, 128, NCH * E], BF16, kind="Internal").ap()
    mtscr = nc.dram_tensor("mtscr", [128, NCH * 1024], BF16, kind="Internal").ap()
    taps = {}
    for t in debug_taps:
        if t in ("x0", "x1", "x2", "x3"):
            taps[t] = nc.dram_tensor("tap_" + t, [SEQ, NST, 128, NCH * ST], F32,
                                     kind="ExternalOutput").ap()
        elif t == "tab":
            taps[t] = nc.dram_tensor("tap_tab", [2, 128, 512], F32,
                                     kind="ExternalOutput").ap()
        elif t == "qk":
            taps[t] = nc.dram_tensor("tap_qk", [2, 128, NCH * S], F32,
                                     kind="ExternalOutput").ap()
        elif t == "tb2":
            taps[t] = nc.dram_tensor("tap_tb2", [3, 128, 2048], F32,
                                     kind="ExternalOutput").ap()

    with contextlib.ExitStack() as es:
        p = Pr(nc, es)
        sb = lambda name, shape, dt=F32: es.enter_context(nc.sbuf_tensor(name, shape, dt))

        cpool = sb("cpool", [128, 16])
        identb = sb("identb", [128, 128], BF16)
        identf = sb("identf", [128, 128])
        onesf = sb("onesf", [128, 128])
        onescl = sb("onescl", [128, 1], BF16)
        onesb1 = sb("onesb1", [128, 1], BF16)
        qkbias = sb("qkbias", [128, 16])
        pbias = sb("pbias", [128, 8])
        fbias = sb("fbias", [128, 8])
        statsb = [sb(f"statsb{q}", [4, 1024]) for q in range(SEQ)]
        rowbuf = sb("rowbuf", [128, 2048])
        recipb = rowbuf[:, 1024:2048]
        bcsb = sb("bcsb", [128, 512])
        sttx = sb("sttx", [128, 512], BF16)
        sttq = sb("sttq", [128, 512], BF16)
        tmpP = sb("tmpP", [128, 512])
        tmpQ = sb("tmpQ", [128, 512])
        lngc = sb("lngc", [128, 1])
        wT = sb("wT", [128, NCH * 1024], BF16)

        pS4 = es.enter_context(nc.psum_tensor("pS4", [128, 2048], F32))
        pY2 = es.enter_context(nc.psum_tensor("pY2", [128, 1024], F32))
        pB = es.enter_context(nc.psum_tensor("pB", [128, 512], F32))
        pD = es.enter_context(nc.psum_tensor("pD", [128, 512], F32))
        pS = [pS4[:, i * 512:(i + 1) * 512] for i in range(4)]
        pY = [pY2[:, i * 512:(i + 1) * 512] for i in range(2)]
        pY2b = pY2[:, :].bitcast(BF16)
        pSb = pS4[:, :].bitcast(BF16)

        cvals = [0.125, 0.03125, -0.5, -1.0, float(np.float32(PI_HI / 2))]
        for i, v in enumerate(cvals):
            nc.const_aps.register(F32, float(np.float32(v)), cpool[:, i:i + 1])

        def init_fn(g):
            for i, v in enumerate(cvals):
                g.memset(cpool[:, i:i + 1], float(np.float32(v)))
            g.memset(onesf[:, :], 1.0)
            g.memset(onescl[:, :], float(np.float32(2.0 ** -10)))
            g.memset(onesb1[:, :], 1.0)
            g.memset(identb[:, :], 1.0)
            return g.memset(identf[:, :], 1.0)
        p.e("gpsimd", init_fn, writes=("init",))
        p.e("gpsimd", lambda g: g.affine_select(identb[:, :], identb[:, :], [[-1, 128]],
                                                ALU.is_equal, 0.0, base=0,
                                                channel_multiplier=1),
            reads=("init",), writes=("init",))
        p.e("gpsimd", lambda g: g.affine_select(identf[:, :], identf[:, :], [[-1, 128]],
                                                ALU.is_equal, 0.0, base=0,
                                                channel_multiplier=1),
            reads=("init",), writes=("init",))

        def bias_loads(sy):
            return [
                sy.dma_start(out=AP(qkbias, 0, [[16, 128], [1, 16]]),
                             in_=AP(bqkv.tensor, 0, [[1, 128], [128, 16], [1, 1]]),
                             allow_slow_non_contiguous=True),
                sy.dma_start(out=AP(pbias, 0, [[8, 128], [1, 8]]),
                             in_=AP(bproj.tensor, 0, [[1, 128], [128, 8], [1, 1]]),
                             allow_slow_non_contiguous=True),
                sy.dma_start(out=AP(fbias, 0, [[8, 128], [1, 8]]),
                             in_=AP(bffn.tensor, 0, [[1, 128], [128, 8], [1, 1]]),
                             allow_slow_non_contiguous=True),
                sy.dma_start(out=lngc[0:1, :], in_=gam[0:1, :])]
        p.ed("sync", bias_loads, 4, "LW", writes=("bias", "lngrow"))

        def stats_emit(seq, s, src_ap, oc, par, src_res):
            xb = sttx[:, :]
            sq = sttq[:, :]
            p.e("vector", lambda v: v.tensor_copy(out=xb, in_=src_ap),
                reads=(src_res,), writes=("sttxr",))
            p.e("vector", lambda v: v.tensor_tensor(out=sq, in0=xb, in1=xb, op=ALU.mult),
                reads=("sttxr",), writes=("sttqr",))
            p.e("tensor", lambda t, oc=oc: t.matmul(pB[64:65, :], onescl[:, :], xb,
                                                    start=(oc == 0), stop=(oc == NCH - 1),
                                                    skip_group_check=True),
                reads=("sttxr", "init"),
                writes=("pb",) if oc in (0, NCH - 1) else ())
            p.e("tensor", lambda t, oc=oc: t.matmul(pB[32:33, :], onescl[:, :], sq,
                                                    start=(oc == 0), stop=(oc == NCH - 1),
                                                    skip_group_check=True),
                reads=("sttqr",),
                writes=("pb",) if oc in (0, NCH - 1) else ())

        def stats_row_store(seq, s):
            def f(v):
                v.tensor_copy(out=rowbuf[0:1, 0:512], in_=pB[64:65, :])
                return v.tensor_copy(out=rowbuf[0:1, 512:1024], in_=pB[32:33, :])
            p.e("vector", f, reads=("pb",), writes=("rowbuf",))
            p.e("scalar", lambda sc: sc.dma_start(out=statsb[seq][s:s + 1, :],
                                                  in_=rowbuf[0:1, 0:1024]),
                dma="SR", reads=("rowbuf",), writes=(f"stats{seq}_{s}",))

        def stats_finalize(seq):
            m = statsb[seq][:, 0:512]
            x2 = statsb[seq][:, 512:1024]
            tmp4 = rowbuf[0:4, 1536:2048]
            allr = tuple(f"stats{seq}_{s}" for s in range(NST))
            p.e("vector", lambda v: v.tensor_tensor(out=tmp4[:, :], in0=m, in1=m,
                                                    op=ALU.mult),
                reads=allr + ("rowbuf",), writes=("rowbuf",))
            p.e("vector", lambda v: v.tensor_tensor(out=tmp4[:, :], in0=x2,
                                                    in1=tmp4[:, :], op=ALU.subtract),
                reads=allr + ("rowbuf",), writes=("rowbuf",))
            p.e("vector", lambda v: v.tensor_scalar_add(out=tmp4[:, :],
                                                        in0=tmp4[:, :],
                                                        scalar1=float(EPS)),
                reads=("rowbuf",), writes=("rowbuf",))
            p.e("scalar", lambda sc: sc.activation(out=tmp4[:, :], in_=tmp4[:, :],
                                                   func=AF.Ln),
                reads=("rowbuf", "init"), writes=("rowbuf",))
            p.e("scalar", lambda sc: sc.activation(out=statsb[seq][:, 512:1024],
                                                   in_=tmp4[:, :],
                                                   func=AF.Exp, scale=-0.5),
                reads=("rowbuf",), writes=(f"rstd{seq}",))

        def prepass():
            with contextlib.ExitStack() as pes:
                xt = pes.enter_context(nc.sbuf_tensor("pp_xt", [128, 2 * 1024], F32))
                xfm = pes.enter_context(nc.sbuf_tensor("pp_xfm", [128, 2 * NCH * ST], F32))
                for seq in range(SEQ):
                    for s in range(NST):
                        par = s % 2
                        stripe = xfm[:, par * NCH * ST:(par + 1) * NCH * ST]
                        for ti in range(4):
                            tc = s * 4 + ti
                            tpar = tc % 2
                            xtt = xt[:, tpar * 1024:(tpar + 1) * 1024]
                            p.e("sync", lambda sy, seq=seq, tc=tc, xtt=xtt:
                                sy.dma_start(out=xtt,
                                             in_=x_in[seq, tc * 128:(tc + 1) * 128, :]),
                                dma=f"LX{tpar}", writes=(f"ppxt{tpar}",))
                            for dc in range(NCH):
                                bank = dc % 2
                                p.e("tensor", lambda t, xtt=xtt, dc=dc, bank=bank:
                                    t.transpose(pS[bank][:, 0:128],
                                                xtt[:, dc * 128:(dc + 1) * 128],
                                                identf[:, :]),
                                    reads=(f"ppxt{tpar}", "init"),
                                    writes=(f"ps{bank}",))
                                p.e("vector", lambda v, stripe=stripe, dc=dc, ti=ti, bank=bank:
                                    v.tensor_copy(out=stripe[:, dc * 512 + ti * 128:
                                                             dc * 512 + ti * 128 + 128],
                                                  in_=pS[bank][:, 0:128]),
                                    reads=(f"ps{bank}",), writes=(f"wbx{par}",))
                        for oc in range(NCH):
                            stats_emit(seq, s, stripe[:, oc * 512:(oc + 1) * 512], oc,
                                       par, f"wbx{par}")
                        stats_row_store(seq, s)
                        p.e("scalar", lambda sc, seq=seq, s=s, stripe=stripe:
                            sc.dma_start(out=xscr[seq, s], in_=stripe),
                            dma="SX0" if par == 0 else "SX1", reads=(f"wbx{par}",),
                            writes=(f"xscr{seq}_{s}",))
                        if "x0" in taps:
                            p.e("scalar", lambda sc, seq=seq, s=s, stripe=stripe:
                                sc.dma_start(out=taps["x0"][seq, s], in_=stripe),
                                dma="SO", reads=(f"wbx{par}",), writes=())
                    stats_finalize(seq)
            p.barrier()

        prepass()

        def build_tables(cos_tm, sin_tm):
            tw = rowbuf
            def ms(g):
                g.memset(tw[0:1, 0:1], 1.0)
                g.memset(tw[0:1, 1:2], float(R32_HI))
                g.memset(tw[0:1, 2:32], 0.0)
                g.memset(tw[0:1, 32:33], 0.0)
                g.memset(tw[0:1, 33:34], float(R32_LO))
                g.memset(tw[0:1, 34:64], 0.0)
                return g.iota(tmpP[:, :], [[128, 16], [0, 32]], base=0,
                              channel_multiplier=1,
                              allow_small_or_imprecise_dtypes=True)
            p.e("gpsimd", ms, writes=("rowbuf", "tmpP"))

            CST = 4097.0

            def tw_mul(dst_hi, dst_lo, ah, al, sh, sl):
                n = ah.shape[1]
                t1 = tw[0:1, 64:64 + n]; t2 = tw[0:1, 96:96 + n]
                u1 = tw[0:1, 128:128 + n]; u2 = tw[0:1, 160:160 + n]
                e1 = tw[0:1, 192:192 + n]
                s1 = tw[0:1, 230:231]; s2 = tw[0:1, 231:232]
                sc1 = tw[0:1, 232:233]; sc2 = tw[0:1, 233:234]
                def E(f):
                    p.e("vector", f, reads=("rowbuf",), writes=("rowbuf",), hard=True)
                E(lambda v: v.tensor_scalar_mul(out=u1, in0=ah, scalar1=CST))
                E(lambda v: v.tensor_tensor(out=u2, in0=u1, in1=ah, op=ALU.subtract))
                E(lambda v: v.tensor_tensor(out=t1, in0=u1, in1=u2, op=ALU.subtract))
                E(lambda v: v.tensor_tensor(out=t2, in0=ah, in1=t1, op=ALU.subtract))
                E(lambda v: v.tensor_scalar_mul(out=s1, in0=sh, scalar1=CST))
                E(lambda v: v.tensor_tensor(out=s2, in0=s1, in1=sh, op=ALU.subtract))
                E(lambda v: v.tensor_tensor(out=sc1, in0=s1, in1=s2, op=ALU.subtract))
                E(lambda v: v.tensor_tensor(out=sc2, in0=sh, in1=sc1, op=ALU.subtract))
                E(lambda v: v.tensor_scalar(out=e1, in0=t1, scalar1=sc1, scalar2=None,
                                            op0=ALU.mult))
                E(lambda v: v.tensor_scalar(out=u2, in0=ah, scalar1=sh, scalar2=None,
                                            op0=ALU.mult))
                E(lambda v: v.tensor_tensor(out=e1, in0=e1, in1=u2, op=ALU.subtract))
                E(lambda v: v.tensor_scalar(out=u1, in0=t1, scalar1=sc2, scalar2=None,
                                            op0=ALU.mult))
                E(lambda v: v.tensor_tensor(out=e1, in0=e1, in1=u1, op=ALU.add))
                E(lambda v: v.tensor_scalar(out=u1, in0=t2, scalar1=sc1, scalar2=None,
                                            op0=ALU.mult))
                E(lambda v: v.tensor_tensor(out=e1, in0=e1, in1=u1, op=ALU.add))
                E(lambda v: v.tensor_scalar(out=u1, in0=t2, scalar1=sc2, scalar2=None,
                                            op0=ALU.mult))
                E(lambda v: v.tensor_tensor(out=e1, in0=e1, in1=u1, op=ALU.add))
                E(lambda v: v.tensor_scalar(out=u1, in0=ah, scalar1=sl, scalar2=None,
                                            op0=ALU.mult))
                E(lambda v: v.tensor_tensor(out=e1, in0=e1, in1=u1, op=ALU.add))
                E(lambda v: v.tensor_scalar(out=u1, in0=al, scalar1=sh, scalar2=None,
                                            op0=ALU.mult))
                E(lambda v: v.tensor_tensor(out=e1, in0=e1, in1=u1, op=ALU.add))
                E(lambda v: v.tensor_tensor(out=u1, in0=u2, in1=e1, op=ALU.add))
                E(lambda v: v.tensor_tensor(out=t1, in0=u1, in1=u2, op=ALU.subtract))
                E(lambda v: v.tensor_tensor(out=dst_lo, in0=e1, in1=t1, op=ALU.subtract))
                E(lambda v: v.tensor_copy(out=dst_hi, in_=u1))

            def pinit(g):
                g.memset(tw[0:1, 228:229], float(R32_HI))
                return g.memset(tw[0:1, 229:230], float(R32_LO))
            p.e("gpsimd", pinit, reads=("rowbuf",), writes=("rowbuf",))
            P_hi = tw[0:1, 228:229]; P_lo = tw[0:1, 229:230]
            for n in (1, 2, 4, 8, 16):
                tw_mul(tw[0:1, n:2 * n], tw[0:1, 32 + n:32 + 2 * n],
                       tw[0:1, 0:n], tw[0:1, 32:32 + n], P_hi, P_lo)
                if n != 16:
                    tw_mul(tw[0:1, 234:235], tw[0:1, 235:236], P_hi, P_lo,
                           P_hi, P_lo)
                    p.e("vector", lambda v: v.tensor_copy(out=P_hi,
                                                           in_=tw[0:1, 234:235]),
                        reads=("rowbuf",), writes=("rowbuf",), hard=True)
                    p.e("vector", lambda v: v.tensor_copy(out=P_lo,
                                                          in_=tw[0:1, 235:236]),
                        reads=("rowbuf",), writes=("rowbuf",), hard=True)
            def arep(v):
                inst = None
                for tc in range(16):
                    inst = v.tensor_copy(out=tw[0:1, 1024 + tc * 32:1024 + (tc + 1) * 32],
                                         in_=tw[0:1, 0:32])
                return inst
            p.e("vector", arep, reads=("rowbuf",), writes=("rowbuf",), hard=True)
            if "tb2" in taps:
                p.e("scalar", lambda sc: sc.dma_start(out=taps["tb2"][0], in_=tw[:, :]),
                    dma="SO", reads=("rowbuf",))
            p.e("tensor", lambda t: t.matmul(pB[:, :], onesf[0:1, :], tw[0:1, 1024:1536],
                                             start=True, stop=True, skip_group_check=True),
                reads=("rowbuf", "init"), writes=("pb",))
            p.e("vector", lambda v: v.tensor_tensor(out=tmpQ[:, :], in0=tmpP[:, :],
                                                    in1=pB[:, :], op=ALU.mult),
                reads=("tmpP", "pb"), writes=("tmpQ",), hard=True)
            if "tb2" in taps:
                p.e("scalar", lambda sc: sc.dma_start(out=taps["tb2"][1][:, 0:512],
                                                      in_=tmpQ[:, :]),
                    dma="SO", reads=("tmpQ",))
            p.e("vector", lambda v: v.tensor_copy(out=tmpP[:, :], in_=tmpQ[:, :]),
                reads=("tmpQ",), writes=("tmpP",), hard=True)
            for k in range(21, -1, -1):
                M = float(np.float32(TWOPI_HI * np.float32(2.0 ** k)))
                p.e("vector", lambda v, M=M:
                    v.tensor_single_scalar(out=bcsb[:, :], in_=tmpP[:, :], scalar=M,
                                           op=ALU.is_ge),
                    reads=("tmpP", "bcsb"), writes=("bcsb",))
                p.e("vector", lambda v, M=M:
                    v.scalar_tensor_tensor(out=tmpP[:, :], in0=bcsb[:, :],
                                           scalar=-M, in1=tmpP[:, :],
                                           op0=ALU.mult, op1=ALU.add),
                    reads=("tmpP", "bcsb"), writes=("tmpP",))
            def FE(f):
                p.e("vector", f, reads=("tmpP", "tmpQ", "bcsb"),
                    writes=("tmpP", "tmpQ", "bcsb"))
            FE(lambda v: v.tensor_single_scalar(out=bcsb[:, :], in_=tmpP[:, :],
                                                scalar=float(PI_HI), op=ALU.is_gt))
            FE(lambda v: v.scalar_tensor_tensor(out=tmpP[:, :], in0=bcsb[:, :],
                                                scalar=-float(TWOPI_HI), in1=tmpP[:, :],
                                                op0=ALU.mult, op1=ALU.add))
            FE(lambda v: v.tensor_tensor(out=bcsb[:, :], in0=tmpQ[:, :], in1=tmpP[:, :],
                                         op=ALU.subtract))
            FE(lambda v: v.tensor_scalar_mul(out=bcsb[:, :], in0=bcsb[:, :],
                                             scalar1=float(INV2PI)))
            FE(lambda v: v.scalar_tensor_tensor(out=tmpP[:, :], in0=bcsb[:, :],
                                                scalar=-float(D_HI), in1=tmpP[:, :],
                                                op0=ALU.mult, op1=ALU.add))
            FE(lambda v: v.scalar_tensor_tensor(out=tmpP[:, :], in0=bcsb[:, :],
                                                scalar=-float(D_LO), in1=tmpP[:, :],
                                                op0=ALU.mult, op1=ALU.add))
            FE(lambda v: v.tensor_single_scalar(out=bcsb[:, :], in_=tmpP[:, :],
                                                scalar=float(PI_HI), op=ALU.is_gt))
            FE(lambda v: v.scalar_tensor_tensor(out=tmpP[:, :], in0=bcsb[:, :],
                                                scalar=-float(TWOPI_HI), in1=tmpP[:, :],
                                                op0=ALU.mult, op1=ALU.add))
            FE(lambda v: v.scalar_tensor_tensor(out=tmpP[:, :], in0=bcsb[:, :],
                                                scalar=-float(D_HI), in1=tmpP[:, :],
                                                op0=ALU.mult, op1=ALU.add))
            FE(lambda v: v.tensor_single_scalar(out=bcsb[:, :], in_=tmpP[:, :],
                                                scalar=-float(PI_HI), op=ALU.is_lt))
            FE(lambda v: v.scalar_tensor_tensor(out=tmpP[:, :], in0=bcsb[:, :],
                                                scalar=float(TWOPI_HI), in1=tmpP[:, :],
                                                op0=ALU.mult, op1=ALU.add))
            FE(lambda v: v.scalar_tensor_tensor(out=tmpP[:, :], in0=bcsb[:, :],
                                                scalar=float(D_HI), in1=tmpP[:, :],
                                                op0=ALU.mult, op1=ALU.add))
            from math import factorial
            CS = [float(np.float32((-1) ** k / factorial(2 * k + 1)))
                  for k in range(11)]
            CC = [float(np.float32((-1) ** k / factorial(2 * k)))
                  for k in range(12)]
            def PE_(f, rd, wr):
                p.e("vector", f, reads=rd, writes=wr)
            PE_(lambda v: v.tensor_tensor(out=bcsb[:, :], in0=tmpP[:, :],
                                          in1=tmpP[:, :], op=ALU.mult),
                ("tmpP", "bcsb"), ("bcsb",))
            PE_(lambda v: v.tensor_scalar(out=sin_tm[:, :], in0=bcsb[:, :],
                                          scalar1=CS[10], scalar2=CS[9],
                                          op0=ALU.mult, op1=ALU.add),
                ("bcsb", "sintm"), ("sintm",))
            for k in range(8, -1, -1):
                PE_(lambda v: v.tensor_tensor(out=sin_tm[:, :], in0=sin_tm[:, :],
                                              in1=bcsb[:, :], op=ALU.mult),
                    ("bcsb", "sintm"), ("sintm",))
                PE_(lambda v, k=k: v.tensor_scalar_add(out=sin_tm[:, :],
                                                       in0=sin_tm[:, :],
                                                       scalar1=CS[k]),
                    ("sintm",), ("sintm",))
            PE_(lambda v: v.tensor_tensor(out=sin_tm[:, :], in0=sin_tm[:, :],
                                          in1=tmpP[:, :], op=ALU.mult),
                ("sintm", "tmpP"), ("sintm",))
            PE_(lambda v: v.tensor_scalar(out=cos_tm[:, :], in0=bcsb[:, :],
                                          scalar1=CC[11], scalar2=CC[10],
                                          op0=ALU.mult, op1=ALU.add),
                ("bcsb", "costm"), ("costm",))
            for k in range(9, -1, -1):
                PE_(lambda v: v.tensor_tensor(out=cos_tm[:, :], in0=cos_tm[:, :],
                                              in1=bcsb[:, :], op=ALU.mult),
                    ("bcsb", "costm"), ("costm",))
                PE_(lambda v, k=k: v.tensor_scalar_add(out=cos_tm[:, :],
                                                       in0=cos_tm[:, :],
                                                       scalar1=CC[k]),
                    ("costm",), ("costm",))
            if "tab" in taps:
                def tp(sc):
                    return [sc.dma_start(out=taps["tab"][0], in_=cos_tm[:, :]),
                            sc.dma_start(out=taps["tab"][1], in_=sin_tm[:, :])]
                p.ed("scalar", tp, 2, "SO", reads=("costm", "sintm"))

        cur = {}

        def ln_load(seq, s):
            xin = cur["xin"]
            par = s % 2
            xin_ = xin[:, par * NCH * ST:(par + 1) * NCH * ST]
            rb = rowbuf[0:1, 0:1024]
            p.e("sync", lambda sy, seq=seq, s=s, xin_=xin_:
                sy.dma_start(out=xin_, in_=xscr[seq, s]),
                dma=f"LX{par}", reads=(f"xscr{seq}_{s}",), writes=(f"xin{par}",))
            p.e("sync", lambda sy, seq=seq, s=s, rb=rb:
                sy.dma_start(out=rb[0:1, 0:1024], in_=statsb[seq][s:s + 1, :]),
                dma=f"LM{par}", reads=(f"stats{seq}_{s}", f"rstd{seq}"),
                writes=("rowbuf",))

        def ln_rest(seq, s, store_ln):
            bufLN, xin = cur["bufLN"], cur["xin"]
            lnmr = cur.get("lnmr")
            par = s % 2
            xin_ = xin[:, par * NCH * ST:(par + 1) * NCH * ST]
            lnb = bufLN[:, par * NCH * ST:(par + 1) * NCH * ST]
            rb = rowbuf[0:1, 0:1024]
            def bc(t, rb=rb):
                t.matmul(pB[:, :], onesf[0:1, :], rb[0:1, 0:512], start=True,
                         stop=True, skip_group_check=True)
                return t.matmul(pD[:, :], onesf[0:1, :], rb[0:1, 512:1024],
                                start=True, stop=True, skip_group_check=True)
            p.e("tensor", bc, reads=("rowbuf", "init"), writes=("pb", "pd"))
            if lnmr is not None:
                mb = lnmr[:, 0:512]
                rs = lnmr[:, 512:1024]
                def cp(v):
                    v.tensor_copy(out=mb, in_=pB[:, :])
                    return v.tensor_copy(out=rs, in_=pD[:, :])
                p.e("vector", cp, reads=("pb", "pd"), writes=("lnmr",))
                mrr = ("lnmr",)
            else:
                mb, rs = pB[:, :], pD[:, :]
                mrr = ("pb", "pd")
            for dc in range(NCH):
                p.e("vector", lambda v, xin_=xin_, dc=dc, mb=mb:
                    v.tensor_tensor(out=tmpP[:, :],
                                    in0=xin_[:, dc * 512:(dc + 1) * 512],
                                    in1=mb, op=ALU.subtract),
                    reads=(f"xin{par}",) + mrr + ("tmpP",), writes=("tmpP",))
                p.e("vector", lambda v, lnb=lnb, dc=dc, rs=rs:
                    v.tensor_tensor(out=lnb[:, dc * 512:(dc + 1) * 512],
                                    in0=tmpP[:, :], in1=rs, op=ALU.mult),
                    reads=mrr + ("tmpP",), writes=(f"bufLN{par}",))
            if store_ln:
                p.e("scalar", lambda sc, s=s, lnb=lnb:
                    sc.dma_start(out=lnscr[s], in_=lnb),
                    dma="SL0" if par == 0 else "SL1",
                    reads=(f"bufLN{par}",), writes=(f"lnscr{s}",))

        def ln_build_stripe(seq, s, store_ln):
            ln_load(seq, s)
            ln_rest(seq, s, store_ln)

        def ln_build(seq, store_ln):
            for s in range(NST):
                ln_build_stripe(seq, s, store_ln)

        def ln_reload(s):
            bufLN = cur["bufLN"]
            par = s % 2
            lnb = bufLN[:, par * NCH * ST:(par + 1) * NCH * ST]
            p.e("sync", lambda sy, s=s, lnb=lnb: sy.dma_start(out=lnb, in_=lnscr[s]),
                dma=f"LL{par}", reads=(f"lnscr{s}",), writes=(f"bufLN{par}",))
            return lnb, par

        def fm_proj_group(wT_sb, src_ap, oc, extra_reads):
            def f(t):
                inst = None
                for dc in range(NCH):
                    inst = t.matmul(pD[:, :],
                                    wT_sb[:, dc * 1024 + oc * 128:dc * 1024 + oc * 128 + 128],
                                    src_ap(dc), start=(dc == 0), stop=(dc == NCH - 1),
                                    skip_group_check=True)
                return inst
            p.e("tensor", f, reads=extra_reads, writes=("pd",))

        def writeback(seq, s, produce, bias_col, tap=None, last_sub=False,
                      do_stats=True, xin_loaded=False, wparts=99, defer=None):
            xin, xout = cur["xin"], cur["xout"]
            par = s % 2
            xin_ = xin[:, par * NCH * ST:(par + 1) * NCH * ST]
            if not xin_loaded:
                p.e("sync", lambda sy, seq=seq, s=s, xin_=xin_:
                    sy.dma_start(out=xin_, in_=xscr[seq, s]),
                    dma=f"LX{par}", reads=(f"xscr{seq}_{s}",), writes=(f"xin{par}",))

            def emit_oc(oc):
                if wparts < 1:
                    return
                produce(oc)
                if wparts < 2:
                    return
                dst = xout[:, par * NCH * ST + oc * 512:par * NCH * ST + (oc + 1) * 512]
                src = xin_[:, oc * 512:(oc + 1) * 512]
                if bias_col is not None:
                    p.e("vector", lambda v, dst=dst, src=src, oc=oc:
                        v.scalar_tensor_tensor(out=dst, in0=pD[:, :],
                                               scalar=bias_col[:, oc:oc + 1], in1=src,
                                               op0=ALU.add, op1=ALU.add),
                        reads=("pd", f"xin{par}", "bias"), writes=(f"xin{par}",))
                else:
                    p.e("vector", lambda v, dst=dst, src=src:
                        v.tensor_tensor(out=pD[:, :] if False else dst, in0=pD[:, :],
                                        in1=src, op=ALU.add),
                        reads=("pd", f"xin{par}"), writes=(f"xin{par}",))
                if do_stats and wparts >= 3:
                    stats_emit(seq, s, dst, oc, par, f"xin{par}")

            def emit_stats_tail():
                if do_stats and wparts >= 3:
                    stats_row_store(seq, s)

            if defer is not None:
                for oc in range(NCH):
                    defer.append(lambda oc=oc: emit_oc(oc))
                defer.append(emit_stats_tail)
            else:
                for oc in range(NCH):
                    emit_oc(oc)
                emit_stats_tail()
            if wparts < 4:
                return
            if not last_sub:
                xo = xout[:, par * NCH * ST:(par + 1) * NCH * ST]

                def emit_store():
                    p.e("scalar", lambda sc, seq=seq, s=s, xo=xo:
                        sc.dma_start(out=xscr[seq, s], in_=xo),
                        dma="SX0" if par == 0 else "SX1",
                        reads=(f"xin{par}",), writes=(f"xscr{seq}_{s}",))
                    if tap is not None:
                        p.e("scalar", lambda sc, seq=seq, s=s, xo=xo:
                            sc.dma_start(out=taps[tap][seq, s], in_=xo),
                            dma="SO", reads=(f"xin{par}",), writes=())
                if defer is not None:
                    defer.append(emit_store)
                else:
                    emit_store()
            else:
                outtm = cur["outtm"]
                for tb in range(4):
                    opar = tb % 2
                    for dc in range(NCH):
                        bank = dc % 2
                        srcc = xout[:, par * NCH * ST + dc * 512 + tb * 128:
                                    par * NCH * ST + dc * 512 + tb * 128 + 128]
                        p.e("tensor", lambda t, srcc=srcc, bank=bank:
                            t.transpose(pS[bank][:, 0:128], srcc, identf[:, :]),
                            reads=(f"xin{par}", "init"), writes=(f"ps{bank}",))
                        p.e("vector", lambda v, dc=dc, bank=bank, opar=opar:
                            v.tensor_copy(out=outtm[:, opar * 1024 + dc * 128:
                                                    opar * 1024 + dc * 128 + 128],
                                          in_=pS[bank][:, 0:128]),
                            reads=(f"ps{bank}",), writes=(f"outtm{opar}",))
                    p.e("scalar", lambda sc, seq=seq, s=s, tb=tb, opar=opar:
                        sc.dma_start(out=out_t[seq, s * 512 + tb * 128:
                                               s * 512 + (tb + 1) * 128, :],
                                     in_=outtm[:, opar * 1024:(opar + 1) * 1024]),
                        dma=f"SP{opar}", reads=(f"outtm{opar}",), writes=())

        def load_w_fm(w_ap, col0, ncols):
            def f(g):
                return [g.dma_start(out=wT[:, dc * 1024:dc * 1024 + ncols],
                                    in_=w_ap[dc * 128:(dc + 1) * 128, col0:col0 + ncols])
                        for dc in range(NCH)]
            p.ed("gpsimd", f, NCH, "GW0", writes=("wT",))

        def attn_a_phase():
          with contextlib.ExitStack() as aes:
              asb = lambda name, shape, dt=BF16: aes.enter_context(
                  nc.sbuf_tensor(name, shape, dt))
              bufQ = asb("bufQ", [128, NCH * S])
              bufK = asb("bufK", [128, NCH * S])
              bufV = asb("bufV", [128, 16 * 1040])
              bufLN = asb("bufLN", [128, 2 * NCH * ST])
              bufPT = bufLN
              bufY = asb("bufY", [128, NCH * ST])
              xin = asb("xinA", [128, 2 * NCH * ST], F32)
              qtm = asb("qtm", [128, 2 * 512])
              cos_tm = asb("cos_tm", [128, 512], F32)
              sin_tm = asb("sin_tm", [128, 512], F32)
              wnat = asb("wnatA", [128, 2 * 1024], BF16)
              ystage = qtm
              cur.update(bufLN=bufLN, xin=xin, xout=xin)
              build_tables(cos_tm, sin_tm)
              p.e("gpsimd", lambda g: g.memset(
                  AP(bufV, 64, [[16 * 1040, 128], [1040, 16], [65, 16]]), 1.0),
                  writes=("vones",))
              def prep_wT(w_ap, slot, cached):
                  if cached:
                      p.e("sync", lambda sy: sy.dma_start(out=wT[:, :], in_=wscr[slot]),
                          dma="LW", reads=(f"wscr{slot}",), writes=("wT",))
                      return
                  for oc in range(NCH):
                      par = oc % 2
                      wn = wnat[:, par * 1024:(par + 1) * 1024]
                      p.e("gpsimd", lambda g, oc=oc, wn=wn:
                          g.dma_start(out=wn, in_=w_ap[oc * 128:(oc + 1) * 128, :]),
                          dma=f"GW{par}", writes=(f"wnat{par}",))
                      for dc in range(NCH):
                          bank = dc % 2
                          p.e("tensor", lambda t, wn=wn, dc=dc, bank=bank:
                              t.transpose(pSb[:, bank * 1024:bank * 1024 + 128],
                                          wn[:, dc * 128:(dc + 1) * 128], identb[:, :]),
                              reads=(f"wnat{par}", "init"), writes=(f"ps{bank}",))
                          p.e("vector", lambda v, dc=dc, oc=oc, bank=bank:
                              v.tensor_copy(out=wT[:, dc * 1024 + oc * 128:
                                                   dc * 1024 + oc * 128 + 128],
                                            in_=pSb[:, bank * 1024:bank * 1024 + 128]),
                              reads=(f"ps{bank}",), writes=("wT",))
                  p.e("scalar", lambda sc, slot=slot: sc.dma_start(out=wscr[slot], in_=wT[:, :]),
                      dma=f"SW{slot}", reads=("wT",), writes=(f"wscr{slot}",))



              def qkv_tm_pass(seq, dst, rope, dst_res):
                  pend = [None]
                  for s in range(NST):
                      lnb, par = ln_reload(s)
                      for ti in range(4):
                          tc = s * 4 + ti
                          for ns in range(2):
                              bank = 2 + (tc * 2 + ns) % 2
                              def mm(t, lnb=lnb, ti=ti, ns=ns, bank=bank):
                                  inst = None
                                  for dc in range(NCH):
                                      inst = t.matmul(
                                          pS[bank][:, :],
                                          lnb[:, dc * 512 + ti * 128:dc * 512 + ti * 128 + 128],
                                          wT[:, dc * 1024 + ns * 512:dc * 1024 + (ns + 1) * 512],
                                          start=(dc == 0), stop=(dc == NCH - 1),
                                          skip_group_check=True)
                                  return inst
                              p.e("tensor", mm, reads=(f"bufLN{par}", "wT"),
                                  writes=(f"ps{bank}",))
                              if rope:
                                  if pend[0] is not None:
                                      pend[0]()
                                      pend[0] = None
                                  qpar = (tc * 2 + ns) % 2
                                  qt = qtm[:, qpar * 512:(qpar + 1) * 512]
                                  po = bank * 512
                                  tr_i = AP(pS4, po, [[2048, 128], [64, 8], [2, 32]])
                                  ti_i = AP(pS4, po + 1, [[2048, 128], [64, 8], [2, 32]])
                                  cs_ = AP(cos_tm, tc * 32, [[512, 128], [0, 8], [1, 32]])
                                  sn_ = AP(sin_tm, tc * 32, [[512, 128], [0, 8], [1, 32]])
                                  qo = qpar * 512
                                  tr_o = AP(qtm, qo, [[1024, 128], [64, 8], [2, 32]])
                                  ti_o = AP(qtm, qo + 1, [[1024, 128], [64, 8], [2, 32]])
                                  tP = AP(tmpP, 0, [[512, 128], [32, 8], [1, 32]])
                                  tQ = AP(tmpQ, 0, [[512, 128], [32, 8], [1, 32]])
                                  p.e("vector", lambda v, tr_i=tr_i, cs_=cs_, tP=tP:
                                      v.tensor_tensor(out=tP, in0=tr_i, in1=cs_,
                                                      op=ALU.mult),
                                      reads=(f"ps{bank}", "costm", "tmpP"),
                                      writes=("tmpP",))
                                  p.e("vector", lambda v, ti_i=ti_i, sn_=sn_, tQ=tQ:
                                      v.tensor_tensor(out=tQ, in0=ti_i, in1=sn_,
                                                      op=ALU.mult),
                                      reads=(f"ps{bank}", "sintm", "tmpQ"),
                                      writes=("tmpQ",))
                                  p.e("vector", lambda v, tr_o=tr_o, tP=tP, tQ=tQ:
                                      v.tensor_tensor(out=tr_o, in0=tP, in1=tQ,
                                                      op=ALU.subtract),
                                      reads=("tmpP", "tmpQ", f"qtm{qpar}"),
                                      writes=(f"qtm{qpar}",))
                                  p.e("vector", lambda v, tr_i=tr_i, sn_=sn_, tP=tP:
                                      v.tensor_tensor(out=tP, in0=tr_i, in1=sn_,
                                                      op=ALU.mult),
                                      reads=(f"ps{bank}", "sintm", "tmpP"),
                                      writes=("tmpP",))
                                  p.e("vector", lambda v, ti_i=ti_i, cs_=cs_, tQ=tQ:
                                      v.tensor_tensor(out=tQ, in0=ti_i, in1=cs_,
                                                      op=ALU.mult),
                                      reads=(f"ps{bank}", "costm", "tmpQ"),
                                      writes=("tmpQ",))
                                  p.e("vector", lambda v, ti_o=ti_o, tP=tP, tQ=tQ:
                                      v.tensor_tensor(out=ti_o, in0=tP, in1=tQ,
                                                      op=ALU.add),
                                      reads=("tmpP", "tmpQ", f"qtm{qpar}"),
                                      writes=(f"qtm{qpar}",))
                                  def tp_unit(qt=qt, qpar=qpar, tc=tc, ns=ns):
                                      for j in range(4):
                                          tb = j % 2
                                          p.e("tensor", lambda t, qt=qt, j=j, tb=tb:
                                              t.transpose(pY2b[:, tb * 1024:tb * 1024 + 128],
                                                          qt[:, j * 128:(j + 1) * 128],
                                                          identb[:, :]),
                                              reads=(f"qtm{qpar}", "init"),
                                              writes=(f"py{tb}",))
                                          p.e("vector", lambda v, dst=dst, tc=tc, ns=ns, j=j, tb=tb:
                                              v.tensor_copy(
                                                  out=dst[:, (ns * 4 + j) * S + tc * 128:
                                                          (ns * 4 + j) * S + tc * 128 + 128],
                                                  in_=pY2b[:, tb * 1024:tb * 1024 + 128]),
                                              reads=(f"py{tb}",), writes=(dst_res,))
                                  pend[0] = tp_unit
                              else:
                                  def vev(v, tc=tc, ns=ns, bank=bank):
                                      o_ap = AP(bufV, tc * 1040 + ns * 8 * 65,
                                                [[16 * 1040, 128], [65, 8], [1, 64]])
                                      return v.tensor_copy(out=o_ap, in_=pS[bank][:, :])
                                  p.e("vector", vev, reads=(f"ps{bank}",),
                                      writes=("bufV",))
                  if pend[0] is not None:
                      pend[0]()
                      pend[0] = None

              def attn_a_attention(seq):
                  NSLOT = H * 8

                  def smm_mms(t, s, i):
                      h, g = divmod(i, 8)
                      hb = (h % 2) * 64
                      ch = h // 2
                      b0 = 2 * (g % 2)
                      inst = None
                      for j in range(2):
                          tkc = 2 * g + j
                          inst = t.matmul(
                              pS[b0 + j][:, :],
                              bufK[hb:hb + 64, ch * S + tkc * 128:
                                   ch * S + (tkc + 1) * 128],
                              bufQ[hb:hb + 64, ch * S + s * 512:
                                   ch * S + (s + 1) * 512],
                              start=True, stop=True, tile_position=(hb, 0),
                              skip_group_check=True)
                      return inst

                  def avm_mms(t, i):
                      h, g = divmod(i, 8)
                      ypar = h % 2
                      inst = None
                      for j in range(2):
                          tkc = 2 * g + j
                          inst = t.matmul(
                              pY[ypar][0:65, :],
                              bufV[:, tkc * 1040 + h * 65:
                                   tkc * 1040 + h * 65 + 65],
                              bufPT[:, g * 1024 + j * 512:
                                    g * 1024 + (j + 1) * 512],
                              start=(tkc == 0), stop=(tkc == 15),
                              skip_group_check=True)
                      return inst

                  def emit_slot(s, i):
                      hi, gi = divmod(i, 8) if i < NSLOT else (None, None)
                      reads = []
                      writes = []
                      if i < NSLOT:
                          b0 = 2 * (gi % 2)
                          reads += ["bufQK"]
                          writes += [f"ps{b0}", f"ps{b0 + 1}"]
                      ia = i - 2
                      if ia >= 0:
                          ha, ga = divmod(ia, 8)
                          reads += [f"bufPT{ga}", "bufV", "vones"]
                          if ia == NSLOT - 1:
                              reads += ["bufLN0", "bufLN1"]
                          if ga in (0, 7):
                              writes += [f"py{ha % 2}"]

                      def tf(t, s=s, i=i, ia=ia):
                          inst = None
                          if i < NSLOT:
                              inst = smm_mms(t, s, i)
                          if ia >= 0:
                              inst = avm_mms(t, ia)
                          return inst
                      p.e("tensor", tf, reads=tuple(reads), writes=tuple(writes))
                      if i < NSLOT:
                          b0 = 2 * (gi % 2)
                          exw = (f"bufPT{gi}",) + (("bufLN0", "bufLN1") if i == 0 else ())
                          p.e("scalar", lambda sc, g=gi, b0=b0:
                              sc.activation(out=bufPT[:, g * 1024:(g + 1) * 1024],
                                            in_=pS4[:, (g % 2) * 1024:
                                                    (g % 2) * 1024 + 1024],
                                            func=AF.Exp, scale=0.125),
                              reads=(f"ps{b0}", f"ps{b0 + 1}", "init"),
                              writes=exw)
                      if ia >= 0 and ia % 8 == 7:
                          emit_tail_recip(ia // 8)

                  def emit_tail_recip(h):
                          ypar = h % 2
                          p.e("scalar", lambda sc, ypar=ypar:
                              sc.activation(out=recipb[64:65, 0:512],
                                            in_=pY[ypar][64:65, :], func=AF.Ln),
                              reads=(f"py{ypar}", "init", "rowbuf"),
                              writes=("rowbuf",))
                          p.e("scalar", lambda sc:
                              sc.activation(out=recipb[64:65, 512:1024],
                                            in_=recipb[64:65, 0:512],
                                            func=AF.Exp, scale=-1.0),
                              reads=("rowbuf",), writes=("rowbuf",))

                  def emit_tail_b(h):
                          ypar = h % 2
                          p.e("tensor", lambda t:
                              t.matmul(pB[0:64, :], onesf[64:65, 0:64],
                                       recipb[64:65, 512:1024], start=True, stop=True,
                                       skip_group_check=True),
                              reads=("rowbuf",), writes=("pb",))
                          p.e("vector", lambda v: v.tensor_copy(out=bcsb[0:64, :],
                                                                in_=pB[0:64, :]),
                              reads=("pb",), writes=("bcs",))
                          if h % 2 == 0:
                              p.e("vector", lambda v, h=h, ypar=ypar:
                                  v.tensor_tensor(out=bufY[0:64, (h // 2) * 512:
                                                           (h // 2) * 512 + 512],
                                                  in0=pY[ypar][0:64, :], in1=bcsb[0:64, :],
                                                  op=ALU.mult),
                                  reads=("bcs", f"py{ypar}"),
                                  writes=("bufY",))
                          else:
                              yspar = (h // 2) % 2
                              p.e("vector", lambda v, ypar=ypar, yspar=yspar:
                                  v.tensor_tensor(out=ystage[0:64, yspar * 512:
                                                             (yspar + 1) * 512],
                                                  in0=pY[ypar][0:64, :], in1=bcsb[0:64, :],
                                                  op=ALU.mult),
                                  reads=("bcs", f"py{ypar}"),
                                  writes=(f"qtm{yspar}",))
                              p.e("scalar", lambda sc, h=h, yspar=yspar:
                                  sc.dma_start(out=bufY[64:128, (h // 2) * 512:
                                                        (h // 2) * 512 + 512],
                                               in_=ystage[0:64, yspar * 512:
                                                          (yspar + 1) * 512]),
                                  dma=f"SY{yspar}", reads=(f"qtm{yspar}",),
                                  writes=("bufY",))
                  deferq = []
                  for s in range(NST):
                      for i in range(NSLOT + 2):
                          emit_slot(s, i)
                          if deferq and i >= 1:
                              deferq.pop(0)()
                          if i >= 13 and (i - 13) % 8 == 0:
                              emit_tail_b((i - 13) // 8)
                      emit_tail_b(H - 1)
                      while deferq:
                          deferq.pop(0)()
                      def produce(oc):
                          fm_proj_group(wT, lambda dc: bufY[:, dc * 512:(dc + 1) * 512],
                                        oc, ("wT", "bufY"))
                      writeback(seq, s, produce, None,
                                tap=("x1" if "x1" in taps else None),
                                defer=(deferq if s < NST - 1 else None))

              for seq in range(SEQ):
                  ln_build(seq, store_ln=True)
                  prep_wT(wq, 0, cached=(seq > 0))
                  qkv_tm_pass(seq, bufQ, True, "bufQK")
                  prep_wT(wk, 1, cached=(seq > 0))
                  qkv_tm_pass(seq, bufK, True, "bufQK")
                  if "qk" in taps and seq == 0:
                      def qk_tap(g):
                          return [g.dma_start(out=taps["qk"][0], in_=bufQ[:, :]),
                                  g.dma_start(out=taps["qk"][1], in_=bufK[:, :])]
                      p.ed("gpsimd", qk_tap, 2, "GW0", reads=("bufQK",))
                  prep_wT(wv, 2, cached=(seq > 0))
                  qkv_tm_pass(seq, bufV, False, "bufV")
                  prep_wT(wo, 3, cached=(seq > 0))
                  attn_a_attention(seq)
                  stats_finalize(seq)
              p.barrier()

        if PHASES >= 2:
            attn_a_phase()

        mTp = {}

        def ffn_sublayer(last):
            with contextlib.ExitStack() as fes:
                fsb = lambda name, shape, dt=F32: fes.enter_context(
                    nc.sbuf_tensor(name, shape, dt))
                sfx = "2" if last else "1"
                mT = fsb("mT" + sfx, [128, NCH * 1024], BF16)
                if last:
                    p.e("sync", lambda sy: sy.dma_start(out=mT[:, :],
                                                        in_=mtscr[:, :]),
                        dma="LL0", writes=("mT",))
                else:
                    wacc = fsb("wacc" + sfx, [128, 2 * 2048])
                    stgA = fsb("stgA" + sfx, [128, 2 * 512])
                    stgB = fsb("stgB" + sfx, [128, 2 * 512])
                bufLN = fsb("bufLNf" + sfx, [128, 2 * NCH * ST], BF16)
                xin = fsb("xinF" + sfx, [128, 2 * NCH * ST])
                outtm = fsb("outtm", [128, 2 * 1024]) if last else None
                lnmr = fsb("lnmr" + sfx, [128, 1024])
                cur.update(bufLN=bufLN, xin=xin, xout=xin, outtm=outtm, lnmr=lnmr)

                for var, (Ae, Be) in enumerate((() if last else
                                                ((aexp, bexp), (aexpi, bexpi)))):
                    for kc in range(4):
                        for c in range(4):
                            par = c % 2
                            def ld(g, Ae=Ae, Be=Be, c=c, kc=kc, par=par):
                                return [
                                    g.dma_start(out=stgA[:, par * 512:(par + 1) * 512],
                                                in_=Ae[c, kc * 128:(kc + 1) * 128, :]),
                                    g.dma_start(out=stgB[:, par * 512:(par + 1) * 512],
                                                in_=Be[c, kc * 128:(kc + 1) * 128, :])]
                            p.ed("gpsimd", ld, 2, f"GS{par}", writes=(f"stg{par}",))
                            wslice = wacc[:, var * 2048 + kc * 512:
                                          var * 2048 + (kc + 1) * 512]
                            if c == 0:
                                p.e("vector", lambda v, wslice=wslice, par=par:
                                    v.tensor_tensor(out=wslice,
                                                    in0=stgA[:, par * 512:(par + 1) * 512],
                                                    in1=stgB[:, par * 512:(par + 1) * 512],
                                                    op=ALU.mult),
                                    reads=(f"stg{par}",),
                                    writes=(f"wacc{var}_{kc}",))
                            else:
                                p.e("vector", lambda v, par=par:
                                    v.tensor_tensor(out=tmpP[:, :],
                                                    in0=stgA[:, par * 512:(par + 1) * 512],
                                                    in1=stgB[:, par * 512:(par + 1) * 512],
                                                    op=ALU.mult),
                                    reads=(f"stg{par}", "tmpP"), writes=("tmpP",))
                                p.e("vector", lambda v, wslice=wslice:
                                    v.tensor_tensor(out=wslice, in0=wslice,
                                                    in1=tmpP[:, :], op=ALU.add),
                                    reads=("tmpP", f"wacc{var}_{kc}"),
                                    writes=(f"wacc{var}_{kc}",))
                for kc in (() if last else range(4)):
                    wr = wacc[:, kc * 512:(kc + 1) * 512]
                    wi = wacc[:, 2048 + kc * 512:2048 + (kc + 1) * 512]
                    def asm(v, kc=kc, wr=wr, wi=wi):
                        v.tensor_copy(out=mT[:, kc * 1024:kc * 1024 + 512], in_=wr)
                        v.tensor_copy(out=mT[:, kc * 1024 + 512:kc * 1024 + 1024], in_=wi)
                        v.tensor_scalar_mul(out=mT[:, (4 + kc) * 1024:
                                                   (4 + kc) * 1024 + 512],
                                            in0=wi, scalar1=-1.0)
                        return v.tensor_copy(out=mT[:, (4 + kc) * 1024 + 512:
                                                    (4 + kc) * 1024 + 1024], in_=wr)
                    p.e("vector", asm, reads=(f"wacc0_{kc}", f"wacc1_{kc}"),
                        writes=("mT",))
                if not last:
                    p.e("scalar", lambda sc: sc.dma_start(out=mtscr[:, :],
                                                          in_=mT[:, :]),
                        dma="SL0", reads=("mT",), writes=())

                for seq in range(SEQ):
                    if FFN_PARTS < 2:
                        break
                    ln_build_stripe(seq, 0, store_ln=False)
                    for s in range(NST):
                        par = s % 2
                        if s + 1 < NST:
                            ln_build_stripe(seq, s + 1, store_ln=False)
                        if FFN_PARTS < 3:
                            continue
                        def produce(oc, par=par):
                            fm_proj_group(
                                mT,
                                lambda dc: bufLN[:, par * NCH * ST + dc * 512:
                                                 par * NCH * ST + (dc + 1) * 512],
                                oc, ("mT", f"bufLN{par}"))
                        writeback(seq, s, produce, fbias,
                                  tap=("x2" if ("x2" in taps and not last) else None),
                                  last_sub=last, do_stats=not last, xin_loaded=True)
                    if not last:
                        stats_finalize(seq)
            p.barrier()

        if PHASES >= 3:
            ffn_sublayer(last=False)

        def attn_b_phase():
          with contextlib.ExitStack() as bes:
              bsb = lambda name, shape, dt=BF16: bes.enter_context(
                  nc.sbuf_tensor(name, shape, dt))
              bufQ = bsb("bufQb", [128, NCH * S])
              bufK = bsb("bufKb", [128, NCH * S])
              bufV = bsb("bufVb", [128, 16 * 1024])
              bufLN = bsb("bufLNb", [128, 2 * NCH * ST])
              bufPT = bufLN
              bufY = bsb("bufYb", [128, NCH * ST])
              xin = bsb("xinB", [128, 2 * NCH * ST], F32)
              vbias = bsb("vbias", [128, 512], F32)
              vbias2 = tmpQ
              dmat = bsb("dmat", [128, 128], F32)
              cur.update(bufLN=bufLN, xin=xin, xout=xin, lnmr=None)
              def vb_ld(sy):
                  return [sy.dma_start(out=vbias[0:1, :],
                                       in_=bqkv[0:1, 2048:2560]),
                          sy.dma_start(out=vbias2[0:1, :],
                                       in_=bqkv[0:1, 2560:3072])]
              p.ed("sync", vb_ld, 2, "LW", reads=("bias",), writes=("vbiasrow",))

              def decay_init(g):
                  return g.iota(dmat[:, :], [[1, 128]], base=0, channel_multiplier=-1,
                                allow_small_or_imprecise_dtypes=True)
              p.e("gpsimd", decay_init, writes=("dmat",))
              p.e("scalar", lambda sc: sc.activation(out=lngc[0:1, :], in_=lngc[0:1, :],
                                                     func=AF.Ln),
                  reads=("lngrow", "init"), writes=("lngrow",))
              p.e("tensor", lambda t: t.matmul(pB[:, 0:1], onesf[0:1, :], lngc[0:1, 0:1],
                                               start=True, stop=True,
                                               skip_group_check=True),
                  reads=("lngrow",), writes=("pb",))
              p.e("vector", lambda v: v.tensor_copy(out=lngc[:, :], in_=pB[:, 0:1]),
                  reads=("pb",), writes=("lngcol",))
              p.e("scalar", lambda sc: sc.activation(out=dmat[:, :], in_=dmat[:, :],
                                                     func=AF.Exp, scale=lngc[:, 0:1]),
                  reads=("dmat", "lngcol"), writes=("dmat",))
              p.e("gpsimd", lambda g: g.affine_select(dmat[:, :], dmat[:, :], [[1, 128]],
                                                      ALU.is_ge, 0.0, base=0,
                                                      channel_multiplier=-1),
                  reads=("dmat",), writes=("dmat",))
              def vb_bc(t):
                  t.matmul(pB[:, :], onesf[0:1, :], vbias[0:1, 0:512], start=True,
                           stop=True, skip_group_check=True)
                  return t.matmul(pD[:, :], onesf[0:1, :], vbias2[0:1, 0:512],
                                  start=True, stop=True, skip_group_check=True)
              p.e("tensor", vb_bc, reads=("vbiasrow", "init"), writes=("pb", "pd"))
              def vb_cp(v):
                  v.tensor_copy(out=vbias[:, 0:512], in_=pB[:, :])
                  return v.tensor_copy(out=vbias2[:, 0:512], in_=pD[:, :])
              p.e("vector", vb_cp, reads=("pb", "pd", "vbiasrow"),
                  writes=("vbias",))
              def qk_fm_pass(seq, dst, bias0):
                  for s in range(NST):
                      lnb, par = ln_reload(s)
                      for oc in range(NCH):
                          bank = 2 + oc % 2
                          def mm(t, lnb=lnb, oc=oc, bank=bank):
                              inst = None
                              for dc in range(NCH):
                                  inst = t.matmul(
                                      pS[bank][:, :],
                                      wT[:, dc * 1024 + oc * 128:dc * 1024 + oc * 128 + 128],
                                      lnb[:, dc * 512:(dc + 1) * 512],
                                      start=(dc == 0), stop=(dc == NCH - 1),
                                      skip_group_check=True)
                              return inst
                          p.e("tensor", mm, reads=(f"bufLN{par}", "wT"),
                              writes=(f"ps{bank}",))
                          p.e("vector", lambda v, dst=dst, s=s, oc=oc, bank=bank,
                              bias0=bias0:
                              v.tensor_scalar(out=dst[:, oc * S + s * 512:
                                                      oc * S + (s + 1) * 512],
                                              in0=pS[bank][:, :],
                                              scalar1=qkbias[:, bias0 + oc:bias0 + oc + 1],
                                              scalar2=None, op0=ALU.add),
                              reads=(f"ps{bank}", "bias"),
                              writes=("bufQK",))

              def vb_tm_pass(seq):
                  for s in range(NST):
                      lnb, par = ln_reload(s)
                      for ti in range(4):
                          tc = s * 4 + ti
                          for ns in range(2):
                              bank = 2 + (tc * 2 + ns) % 2
                              def mm(t, lnb=lnb, ti=ti, ns=ns, bank=bank):
                                  inst = None
                                  for dc in range(NCH):
                                      inst = t.matmul(
                                          pS[bank][:, :],
                                          lnb[:, dc * 512 + ti * 128:
                                              dc * 512 + ti * 128 + 128],
                                          wT[:, dc * 1024 + ns * 512:
                                             dc * 1024 + (ns + 1) * 512],
                                          start=(dc == 0), stop=(dc == NCH - 1),
                                          skip_group_check=True)
                                  return inst
                              p.e("tensor", mm, reads=(f"bufLN{par}", "wT"),
                                  writes=(f"ps{bank}",))
                              def vev(v, tc=tc, ns=ns, bank=bank):
                                  return v.scalar_tensor_tensor(
                                      out=bufV[:, tc * 1024 + ns * 512:
                                               tc * 1024 + (ns + 1) * 512],
                                      in0=pS[bank][:, :], scalar=1.0,
                                      in1=(vbias if ns == 0 else vbias2)[:, 0:512],
                                      op0=ALU.mult, op1=ALU.add)
                              p.e("vector", vev, reads=(f"ps{bank}", "vbias"),
                                  writes=("bufV",))

              def attn_b_attention(seq):
                  for s in range(NST):
                      if B_ATT < 1:
                          break
                      for tkc in range(16):
                          bank = tkc % 2
                          def mm(t, s=s, tkc=tkc, bank=bank):
                              inst = None
                              for dc in range(NCH):
                                  inst = t.matmul(
                                      pS[bank][:, :],
                                      bufK[:, dc * S + tkc * 128:dc * S + (tkc + 1) * 128],
                                      bufQ[:, dc * S + s * 512:dc * S + (s + 1) * 512],
                                      start=(dc == 0), stop=(dc == NCH - 1),
                                      skip_group_check=True)
                              return inst
                          p.e("tensor", mm, reads=("bufQK",), writes=(f"ps{bank}",))
                          if B_ATT >= 2 and s * 4 <= tkc < s * 4 + 4:
                              off = (tkc - s * 4) * 128
                              p.e("vector", lambda v, bank=bank, off=off:
                                  v.tensor_tensor(out=pS[bank][:, off:off + 128],
                                                  in0=pS[bank][:, off:off + 128],
                                                  in1=dmat[:, :], op=ALU.mult),
                                  reads=(f"ps{bank}", "dmat"), writes=(f"ps{bank}",))
                          p.e("scalar", lambda sc, tkc=tkc, bank=bank:
                              sc.activation(out=bufPT[:, tkc * 512:(tkc + 1) * 512],
                                            in_=pS[bank][:, :], func=AF.Exp,
                                            scale=0.03125),
                              reads=(f"ps{bank}", "init"),
                              writes=(f"bufLN{tkc // 8}",))
                      if B_ATT < 3:
                          continue
                      def den(t):
                          inst = None
                          for tkc in range(16):
                              inst = t.matmul(pY[0][0:1, :], onesb1[:, :],
                                              bufPT[:, tkc * 512:(tkc + 1) * 512],
                                              start=(tkc == 0), stop=(tkc == 15),
                                              skip_group_check=True)
                          return inst
                      p.e("tensor", den, reads=("bufLN0", "bufLN1"),
                          writes=("py0",))
                      p.e("scalar", lambda sc:
                          sc.activation(out=recipb[0:1, 0:512], in_=pY[0][0:1, :],
                                        func=AF.Ln),
                          reads=("py0", "init", "rowbuf"), writes=("rowbuf",))
                      p.e("scalar", lambda sc:
                          sc.activation(out=recipb[0:1, 512:1024],
                                        in_=recipb[0:1, 0:512], func=AF.Exp,
                                        scale=-1.0),
                          reads=("rowbuf",), writes=("rowbuf",))
                      p.e("tensor", lambda t: t.matmul(pB[:, :], onesf[0:1, :],
                                                       recipb[0:1, 512:1024], start=True,
                                                       stop=True, skip_group_check=True),
                          reads=("rowbuf",), writes=("pb",))
                      p.e("vector", lambda v: v.tensor_copy(out=bcsb[:, :], in_=pB[:, :]),
                          reads=("pb",), writes=("bcs",))
                      if B_ATT < 4:
                          continue
                      for oc in range(NCH):
                          ypar = oc % 2
                          def avm(t, oc=oc, ypar=ypar):
                              inst = None
                              for tkc in range(16):
                                  inst = t.matmul(
                                      pY[ypar][:, :],
                                      bufV[:, tkc * 1024 + oc * 128:
                                           tkc * 1024 + (oc + 1) * 128],
                                      bufPT[:, tkc * 512:(tkc + 1) * 512],
                                      start=(tkc == 0), stop=(tkc == 15),
                                      skip_group_check=True)
                              return inst
                          p.e("tensor", avm,
                              reads=("bufLN0", "bufLN1", "bufV"),
                              writes=(f"py{ypar}",))
                          p.e("vector", lambda v, oc=oc, ypar=ypar:
                              v.tensor_tensor(out=bufY[:, oc * 512:(oc + 1) * 512],
                                              in0=pY[ypar][:, :], in1=bcsb[:, :],
                                              op=ALU.mult),
                              reads=("bcs", f"py{ypar}"),
                              writes=("bufY",))
                      if B_ATT < 5:
                          continue
                      def produce(oc):
                          fm_proj_group(wT, lambda dc: bufY[:, dc * 512:(dc + 1) * 512],
                                        oc, ("wT", "bufY"))
                      writeback(seq, s, produce, pbias,
                                tap=("x3" if "x3" in taps else None), wparts=W_PARTS)

              for seq in range(SEQ):
                  if B_PARTS < 2:
                      break
                  ln_build(seq, store_ln=True)
                  if B_PARTS >= 3:
                      load_w_fm(wqkv, 0, 1024)
                      qk_fm_pass(seq, bufQ, 0)
                      load_w_fm(wqkv, 1024, 1024)
                      qk_fm_pass(seq, bufK, 8)
                  if B_PARTS >= 4:
                      load_w_fm(wqkv, 2048, 1024)
                      vb_tm_pass(seq)
                  if B_PARTS >= 5:
                      if B_ATT != -2:
                          load_w_fm(wproj, 0, 1024)
                      attn_b_attention(seq)
                      if B_ATT >= 0:
                          stats_finalize(seq)
              p.barrier()

        if PHASES >= 4:
            attn_b_phase()

        if PHASES >= 5:
            ffn_sublayer(last=True)

        fin = [(s_, p.last(s_)) for s_ in p.cnt if s_ != "Y"]
        p.cnt["Y"] += 1
        p.ops["sync"].append((fin, lambda e: e.nop(), "Y", 1, False))

        p.flush(es.enter_context(nc.Block()))
    return nc


_NC_CACHE = {}


def _get_nc(taps=()):
    key = tuple(taps)
    if key not in _NC_CACHE:
        _NC_CACHE[key] = build_nc(key)
    return _NC_CACHE[key]


def kernel(L0_attn_a_x, L0_attn_a_w_q, L0_attn_a_w_k, L0_attn_a_w_v, L0_attn_a_w_o,
           L0_attn_b_Wqkv, L0_attn_b_bqkv, L0_attn_b_Wproj, L0_attn_b_bproj,
           L0_attn_b_gamma, L0_ffn_G1_R, L0_ffn_G2_R, L0_ffn_G1_I, L0_ffn_G2_I,
           L0_ffn_bias_real, L0_ffn_bias_imag, _collect=None):
    f32 = np.float32
    x = np.ascontiguousarray(np.asarray(L0_attn_a_x, f32))
    G1R = np.asarray(L0_ffn_G1_R, f32); G2R = np.asarray(L0_ffn_G2_R, f32)
    G1I = np.asarray(L0_ffn_G1_I, f32); G2I = np.asarray(L0_ffn_G2_I, f32)

    def expand(G1, G2):
        t1 = np.transpose(G1[0], (1, 2, 0))
        Ae = np.broadcast_to(t1[:, :, None, :, None],
                             (4, 32, 16, 32, 16)).reshape(4, 512, 512)
        t2 = np.transpose(G2[:, :, 0, :], (0, 2, 1))
        Be = np.broadcast_to(t2[:, None, :, None, :],
                             (4, 32, 16, 32, 16)).reshape(4, 512, 512)
        return np.ascontiguousarray(Ae), np.ascontiguousarray(Be)

    AeR, BeR = expand(G1R, G2R)
    AeI, BeI = expand(G1I, G2I)
    shared = {
        "wq": np.ascontiguousarray(np.asarray(L0_attn_a_w_q, f32)),
        "wk": np.ascontiguousarray(np.asarray(L0_attn_a_w_k, f32)),
        "wv": np.ascontiguousarray(np.asarray(L0_attn_a_w_v, f32)),
        "wo": np.ascontiguousarray(np.asarray(L0_attn_a_w_o, f32)),
        "wqkv": np.ascontiguousarray(np.asarray(L0_attn_b_Wqkv, f32)),
        "bqkv": np.asarray(L0_attn_b_bqkv, f32).reshape(1, 3072),
        "wproj": np.ascontiguousarray(np.asarray(L0_attn_b_Wproj, f32)),
        "bproj": np.asarray(L0_attn_b_bproj, f32).reshape(1, 1024),
        "gam": np.asarray(L0_attn_b_gamma, f32).reshape(1, 1),
        "aexp": AeR, "bexp": BeR, "aexpi": AeI, "bexpi": BeI,
        "bffn": np.concatenate([np.asarray(L0_ffn_bias_real, f32),
                                np.asarray(L0_ffn_bias_imag, f32)]).reshape(1, 1024),
    }
    in_maps = []
    for c in range(NCORE):
        m = dict(shared)
        m["x"] = x[c * SEQ:(c + 1) * SEQ]
        in_maps.append(m)
    nc = _get_nc(tuple(DEBUG_TAPS))
    res = run_bass_kernel_spmd(nc, in_maps, core_ids=list(range(NCORE)))
    if _collect is not None:
        _collect.extend(res.results)
    out = np.concatenate([r["out"] for r in res.results], axis=0)
    return out.reshape(16, 2048, 1024)


# version 08
# speedup vs baseline: 1.3912x; 1.0943x over previous
import contextlib
import numpy as np
import ml_dtypes
import concourse.bass as bass
import concourse.mybir as mybir
from concourse.bass_utils import run_bass_kernel_spmd

F32 = mybir.dt.float32
BF16 = mybir.dt.bfloat16
AF = mybir.ActivationFunctionType
ALU = mybir.AluOpType
AP = bass.AP

NCORE = 8
SEQ = 2
S = 2048
E = 1024
H = 16
NST = 4
ST = 512
NCH = 8
EPS = 1e-5

TWOPI_HI = np.float32(2 * np.pi)
D_HI = np.float32(np.float64(2 * np.pi) - np.float64(TWOPI_HI))
D_LO = np.float32(np.float64(2 * np.pi) - np.float64(TWOPI_HI) - np.float64(D_HI))
PI_HI = np.float32(np.pi)
INV2PI = np.float32(1.0 / np.float64(TWOPI_HI))
R32_64 = np.float64(10000.0) ** (np.float64(1) / 32)
R32_HI = np.float32(R32_64)
R32_LO = np.float32(R32_64 - np.float64(R32_HI))

DEBUG_TAPS = []
PHASES = 99
FFN_PARTS = 99
B_PARTS = 99
B_ATT = 99
W_PARTS = 99
SIM_SAFE = False


class Pr:

    ENG = ("sync", "scalar", "vector", "tensor", "gpsimd")
    CSEM = {"scalar": "C", "vector": "V", "tensor": "T", "gpsimd": "G", "sync": "Y"}
    DMA_STREAMS = ("LX0", "LX1", "LM0", "LM1", "LL0", "LL1", "LW",
                   "SX0", "SX1", "SL0", "SL1", "SY0", "SY1", "SR",
                   "SW0", "SW1", "SW2", "SW3", "SP0", "SP1", "SO",
                   "GW0", "GW1", "GS0", "GS1")

    def __init__(self, nc, es):
        self.nc = nc
        self.ops = {e: [] for e in self.ENG}
        self.sems = {}
        self.cnt = {}
        for name in ("C", "V", "T", "G", "Y") + self.DMA_STREAMS:
            self.sems[name] = es.enter_context(nc.semaphore("sem" + name))
            self.cnt[name] = 0
        self.res = {}
        self.gps_prev = None

    def e(self, eng, fn, waits=(), reads=(), writes=(), dma=None, hard=False):
        if dma:
            sem, inc = dma, 16
        else:
            sem, inc = self.CSEM[eng], 1
        w = {}
        if eng == "gpsimd" and self.gps_prev is not None:
            w[self.gps_prev[0]] = max(w.get(self.gps_prev[0], 0), self.gps_prev[1])
        for s_, v in waits:
            if v:
                w[s_] = max(w.get(s_, 0), v)
        for r in reads:
            ent = self.res.get(r)
            if ent:
                for s_, v in ent["w"]:
                    w[s_] = max(w.get(s_, 0), v)
        for r in writes:
            ent = self.res.get(r)
            if ent:
                if ent["r"]:
                    for s_, v in ent["w"]:
                        w[s_] = max(w.get(s_, 0), v)
                    for s_, v in ent["r"]:
                        w[s_] = max(w.get(s_, 0), v)
        self.cnt[sem] += inc
        tok = self.cnt[sem]
        if eng == "gpsimd":
            self.gps_prev = (sem, tok)
        self.ops[eng].append((sorted(w.items()), fn, sem, inc, hard))
        for r in reads:
            self.res.setdefault(r, {"w": [], "r": []})["r"].append((sem, tok))
        for r in writes:
            ent = self.res.setdefault(r, {"w": [], "r": []})
            if ent["r"]:
                ent["w"] = [(sem, tok)]
                ent["r"] = []
            else:
                ent["w"].append((sem, tok))
        return tok

    def ed(self, eng, fnmaker, n, sem, waits=(), reads=(), writes=()):
        w = {}
        if eng == "gpsimd" and self.gps_prev is not None:
            w[self.gps_prev[0]] = max(w.get(self.gps_prev[0], 0), self.gps_prev[1])
        for s_, v in waits:
            if v:
                w[s_] = max(w.get(s_, 0), v)
        for r in reads:
            ent = self.res.get(r)
            if ent:
                for s_, v in ent["w"]:
                    w[s_] = max(w.get(s_, 0), v)
        for r in writes:
            ent = self.res.get(r)
            if ent and ent["r"]:
                for s_, v in ent["w"]:
                    w[s_] = max(w.get(s_, 0), v)
                for s_, v in ent["r"]:
                    w[s_] = max(w.get(s_, 0), v)
        self.cnt[sem] += 16 * n
        tok = self.cnt[sem]
        if eng == "gpsimd":
            self.gps_prev = (sem, tok)
        semh = self.sems[sem]
        def wrapped(e, fnmaker=fnmaker, semh=semh):
            for ins in fnmaker(e):
                ins.then_inc(semh, 16)
            return None
        self.ops[eng].append((sorted(w.items()), wrapped, sem, 0, False))
        for r in reads:
            self.res.setdefault(r, {"w": [], "r": []})["r"].append((sem, tok))
        for r in writes:
            ent = self.res.setdefault(r, {"w": [], "r": []})
            if ent["r"]:
                ent["w"] = [(sem, tok)]
                ent["r"] = []
            else:
                ent["w"].append((sem, tok))
        return tok

    def barrier(self):
        snap = [(s, self.cnt[s]) for s in self.cnt]
        for eng in self.ENG:
            sem = self.CSEM[eng]
            self.cnt[sem] += 1
            self.ops[eng].append((list(snap), lambda e: e.nop(), sem, 1, False))
        self.gps_prev = ("G", self.cnt["G"])
        self.res["__bar__"] = {"w": None, "r": []}

    def last(self, sem):
        return self.cnt[sem]

    def flush(self, block):
        hmap = {"sync": block.sync, "scalar": block.scalar, "vector": block.vector,
                "tensor": block.tensor, "gpsimd": block.gpsimd}
        for eng in self.ENG:
            ops = self.ops[eng]
            if not ops:
                continue

            def body(e, ops=ops, eng=eng):
                seen = {}
                for w, fn, sem, inc, hard in ops:
                    for s_, v in w:
                        if seen.get(s_, 0) < v:
                            e.wait_ge(self.sems[s_], v)
                            seen[s_] = v
                    inst = fn(e)
                    if inc:
                        inst.then_inc(self.sems[sem], inc)
            hmap[eng](body)


def build_nc(debug_taps=()):
    nc = bass.Bass("TRN2", target_bir_lowering=False)

    def din(name, shape, dt=F32):
        return nc.dram_tensor(name, shape, dt, kind="ExternalInput").ap()

    x_in = din("x", [SEQ, S, E])
    wq = din("wq", [E, E]); wk = din("wk", [E, E])
    wv = din("wv", [E, E]); wo = din("wo", [E, E])
    wqkv = din("wqkv", [E, 3 * E])
    bqkv = din("bqkv", [1, 3 * E])
    wproj = din("wproj", [E, E])
    bproj = din("bproj", [1, E])
    gam = din("gam", [1, 1])
    aexp = din("aexp", [4, 512, 512]); bexp = din("bexp", [4, 512, 512])
    aexpi = din("aexpi", [4, 512, 512]); bexpi = din("bexpi", [4, 512, 512])
    bffn = din("bffn", [1, E])

    out_t = nc.dram_tensor("out", [SEQ, S, E], F32, kind="ExternalOutput").ap()
    xscr = nc.dram_tensor("xscr", [SEQ, NST, 128, NCH * ST], F32, kind="Internal").ap()
    lnscr = nc.dram_tensor("lnscr", [NST, 128, NCH * ST], BF16, kind="Internal").ap()
    wscr = nc.dram_tensor("wscr", [4, 128, NCH * E], BF16, kind="Internal").ap()
    mtscr = nc.dram_tensor("mtscr", [128, NCH * 1024], BF16, kind="Internal").ap()
    taps = {}
    for t in debug_taps:
        if t in ("x0", "x1", "x2", "x3"):
            taps[t] = nc.dram_tensor("tap_" + t, [SEQ, NST, 128, NCH * ST], F32,
                                     kind="ExternalOutput").ap()
        elif t == "tab":
            taps[t] = nc.dram_tensor("tap_tab", [2, 128, 512], F32,
                                     kind="ExternalOutput").ap()
        elif t == "qk":
            taps[t] = nc.dram_tensor("tap_qk", [2, 128, NCH * S], F32,
                                     kind="ExternalOutput").ap()
        elif t == "tb2":
            taps[t] = nc.dram_tensor("tap_tb2", [3, 128, 2048], F32,
                                     kind="ExternalOutput").ap()

    with contextlib.ExitStack() as es:
        p = Pr(nc, es)
        sb = lambda name, shape, dt=F32: es.enter_context(nc.sbuf_tensor(name, shape, dt))

        cpool = sb("cpool", [128, 16])
        identb = sb("identb", [128, 128], BF16)
        identf = sb("identf", [128, 128])
        onesf = sb("onesf", [128, 128])
        onescl = sb("onescl", [128, 1], BF16)
        onesb1 = sb("onesb1", [128, 1], BF16)
        qkbias = sb("qkbias", [128, 16])
        pbias = sb("pbias", [128, 8])
        fbias = sb("fbias", [128, 8])
        statsb = [sb(f"statsb{q}", [4, 1024]) for q in range(SEQ)]
        rowbuf = sb("rowbuf", [128, 2048])
        recipb = rowbuf[:, 1024:2048]
        bcsb = sb("bcsb", [128, 512])
        sttx = sb("sttx", [128, 1024], BF16)
        sttq = sb("sttq", [128, 1024], BF16)
        tmpP = sb("tmpP", [128, 512])
        tmpQ = sb("tmpQ", [128, 512])
        lngc = sb("lngc", [128, 1])
        wT = sb("wT", [128, NCH * 1024], BF16)

        pS4 = es.enter_context(nc.psum_tensor("pS4", [128, 2048], F32))
        pY2 = es.enter_context(nc.psum_tensor("pY2", [128, 1024], F32))
        pB = es.enter_context(nc.psum_tensor("pB", [128, 512], F32))
        pD = es.enter_context(nc.psum_tensor("pD", [128, 512], F32))
        pS = [pS4[:, i * 512:(i + 1) * 512] for i in range(4)]
        pY = [pY2[:, i * 512:(i + 1) * 512] for i in range(2)]
        pY2b = pY2[:, :].bitcast(BF16)
        pSb = pS4[:, :].bitcast(BF16)

        cvals = [0.125, 0.03125, -0.5, -1.0, float(np.float32(PI_HI / 2))]
        for i, v in enumerate(cvals):
            nc.const_aps.register(F32, float(np.float32(v)), cpool[:, i:i + 1])

        def init_fn(g):
            for i, v in enumerate(cvals):
                g.memset(cpool[:, i:i + 1], float(np.float32(v)))
            g.memset(onesf[:, :], 1.0)
            g.memset(onescl[:, :], float(np.float32(2.0 ** -10)))
            g.memset(onesb1[:, :], 1.0)
            g.memset(identb[:, :], 1.0)
            return g.memset(identf[:, :], 1.0)
        p.e("gpsimd", init_fn, writes=("init",))
        p.e("gpsimd", lambda g: g.affine_select(identb[:, :], identb[:, :], [[-1, 128]],
                                                ALU.is_equal, 0.0, base=0,
                                                channel_multiplier=1),
            reads=("init",), writes=("init",))
        p.e("gpsimd", lambda g: g.affine_select(identf[:, :], identf[:, :], [[-1, 128]],
                                                ALU.is_equal, 0.0, base=0,
                                                channel_multiplier=1),
            reads=("init",), writes=("init",))

        def bias_loads(sy):
            return [
                sy.dma_start(out=AP(qkbias, 0, [[16, 128], [1, 16]]),
                             in_=AP(bqkv.tensor, 0, [[1, 128], [128, 16], [1, 1]]),
                             allow_slow_non_contiguous=True),
                sy.dma_start(out=AP(pbias, 0, [[8, 128], [1, 8]]),
                             in_=AP(bproj.tensor, 0, [[1, 128], [128, 8], [1, 1]]),
                             allow_slow_non_contiguous=True),
                sy.dma_start(out=AP(fbias, 0, [[8, 128], [1, 8]]),
                             in_=AP(bffn.tensor, 0, [[1, 128], [128, 8], [1, 1]]),
                             allow_slow_non_contiguous=True),
                sy.dma_start(out=lngc[0:1, :], in_=gam[0:1, :])]
        p.ed("sync", bias_loads, 4, "LW", writes=("bias", "lngrow"))

        def stats_dve(seq, s, src_ap, oc, par, src_res):
            sl = oc % 2
            xb = sttx[:, sl * 512:(sl + 1) * 512]
            sq = sttq[:, sl * 512:(sl + 1) * 512]
            p.e("vector", lambda v, xb=xb: v.tensor_copy(out=xb, in_=src_ap),
                reads=(src_res,), writes=(f"sttxr{sl}",))
            p.e("vector", lambda v, xb=xb, sq=sq:
                v.tensor_tensor(out=sq, in0=xb, in1=xb, op=ALU.mult),
                reads=(f"sttxr{sl}",), writes=(f"sttqr{sl}",))

        def stats_mm(oc):
            sl = oc % 2
            xb = sttx[:, sl * 512:(sl + 1) * 512]
            sq = sttq[:, sl * 512:(sl + 1) * 512]
            p.e("tensor", lambda t, oc=oc, xb=xb:
                t.matmul(pB[64:65, :], onescl[:, :], xb,
                         start=(oc == 0), stop=(oc == NCH - 1),
                         skip_group_check=True),
                reads=(f"sttxr{sl}", "init"),
                writes=("pb",) if oc in (0, NCH - 1) else ())
            p.e("tensor", lambda t, oc=oc, sq=sq:
                t.matmul(pB[32:33, :], onescl[:, :], sq,
                         start=(oc == 0), stop=(oc == NCH - 1),
                         skip_group_check=True),
                reads=(f"sttqr{sl}",),
                writes=("pb",) if oc in (0, NCH - 1) else ())

        def stats_emit(seq, s, src_ap, oc, par, src_res):
            stats_dve(seq, s, src_ap, oc, par, src_res)
            stats_mm(oc)

        def stats_row_store(seq, s):
            def f(v):
                v.tensor_copy(out=rowbuf[0:1, 0:512], in_=pB[64:65, :])
                return v.tensor_copy(out=rowbuf[0:1, 512:1024], in_=pB[32:33, :])
            p.e("vector", f, reads=("pb",), writes=("rowbuf",))
            p.e("scalar", lambda sc: sc.dma_start(out=statsb[seq][s:s + 1, :],
                                                  in_=rowbuf[0:1, 0:1024]),
                dma="SR", reads=("rowbuf",), writes=(f"stats{seq}_{s}",))

        def stats_finalize(seq):
            m = statsb[seq][:, 0:512]
            x2 = statsb[seq][:, 512:1024]
            tmp4 = rowbuf[0:4, 1536:2048]
            allr = tuple(f"stats{seq}_{s}" for s in range(NST))
            p.e("vector", lambda v: v.tensor_tensor(out=tmp4[:, :], in0=m, in1=m,
                                                    op=ALU.mult),
                reads=allr + ("rowbuf",), writes=("rowbuf",))
            p.e("vector", lambda v: v.tensor_tensor(out=tmp4[:, :], in0=x2,
                                                    in1=tmp4[:, :], op=ALU.subtract),
                reads=allr + ("rowbuf",), writes=("rowbuf",))
            p.e("vector", lambda v: v.tensor_scalar_add(out=tmp4[:, :],
                                                        in0=tmp4[:, :],
                                                        scalar1=float(EPS)),
                reads=("rowbuf",), writes=("rowbuf",))
            p.e("scalar", lambda sc: sc.activation(out=tmp4[:, :], in_=tmp4[:, :],
                                                   func=AF.Ln),
                reads=("rowbuf", "init"), writes=("rowbuf",))
            p.e("scalar", lambda sc: sc.activation(out=statsb[seq][:, 512:1024],
                                                   in_=tmp4[:, :],
                                                   func=AF.Exp, scale=-0.5),
                reads=("rowbuf",), writes=(f"rstd{seq}",))

        def prepass():
            with contextlib.ExitStack() as pes:
                xt = pes.enter_context(nc.sbuf_tensor("pp_xt", [128, 2 * 1024], F32))
                xfm = pes.enter_context(nc.sbuf_tensor("pp_xfm", [128, 2 * NCH * ST], F32))
                for seq in range(SEQ):
                    for s in range(NST):
                        par = s % 2
                        stripe = xfm[:, par * NCH * ST:(par + 1) * NCH * ST]
                        for ti in range(4):
                            tc = s * 4 + ti
                            tpar = tc % 2
                            xtt = xt[:, tpar * 1024:(tpar + 1) * 1024]
                            p.e("sync", lambda sy, seq=seq, tc=tc, xtt=xtt:
                                sy.dma_start(out=xtt,
                                             in_=x_in[seq, tc * 128:(tc + 1) * 128, :]),
                                dma=f"LX{tpar}", writes=(f"ppxt{tpar}",))
                            for dc in range(NCH):
                                bank = dc % 2
                                p.e("tensor", lambda t, xtt=xtt, dc=dc, bank=bank:
                                    t.transpose(pS[bank][:, 0:128],
                                                xtt[:, dc * 128:(dc + 1) * 128],
                                                identf[:, :]),
                                    reads=(f"ppxt{tpar}", "init"),
                                    writes=(f"ps{bank}",))
                                p.e("vector", lambda v, stripe=stripe, dc=dc, ti=ti, bank=bank:
                                    v.tensor_copy(out=stripe[:, dc * 512 + ti * 128:
                                                             dc * 512 + ti * 128 + 128],
                                                  in_=pS[bank][:, 0:128]),
                                    reads=(f"ps{bank}",), writes=(f"wbx{par}",))
                        for oc in range(NCH):
                            stats_emit(seq, s, stripe[:, oc * 512:(oc + 1) * 512], oc,
                                       par, f"wbx{par}")
                        stats_row_store(seq, s)
                        p.e("scalar", lambda sc, seq=seq, s=s, stripe=stripe:
                            sc.dma_start(out=xscr[seq, s], in_=stripe),
                            dma="SX0" if par == 0 else "SX1", reads=(f"wbx{par}",),
                            writes=(f"xscr{seq}_{s}",))
                        if "x0" in taps:
                            p.e("scalar", lambda sc, seq=seq, s=s, stripe=stripe:
                                sc.dma_start(out=taps["x0"][seq, s], in_=stripe),
                                dma="SO", reads=(f"wbx{par}",), writes=())
                    stats_finalize(seq)
            p.barrier()

        prepass()

        def build_tables(cos_tm, sin_tm):
            tw = rowbuf
            def ms(g):
                g.memset(tw[0:1, 0:1], 1.0)
                g.memset(tw[0:1, 1:2], float(R32_HI))
                g.memset(tw[0:1, 2:32], 0.0)
                g.memset(tw[0:1, 32:33], 0.0)
                g.memset(tw[0:1, 33:34], float(R32_LO))
                g.memset(tw[0:1, 34:64], 0.0)
                return g.iota(tmpP[:, :], [[128, 16], [0, 32]], base=0,
                              channel_multiplier=1,
                              allow_small_or_imprecise_dtypes=True)
            p.e("gpsimd", ms, writes=("rowbuf", "tmpP"))

            CST = 4097.0

            def tw_mul(dst_hi, dst_lo, ah, al, sh, sl):
                n = ah.shape[1]
                t1 = tw[0:1, 64:64 + n]; t2 = tw[0:1, 96:96 + n]
                u1 = tw[0:1, 128:128 + n]; u2 = tw[0:1, 160:160 + n]
                e1 = tw[0:1, 192:192 + n]
                s1 = tw[0:1, 230:231]; s2 = tw[0:1, 231:232]
                sc1 = tw[0:1, 232:233]; sc2 = tw[0:1, 233:234]
                def E(f):
                    p.e("vector", f, reads=("rowbuf",), writes=("rowbuf",), hard=True)
                E(lambda v: v.tensor_scalar_mul(out=u1, in0=ah, scalar1=CST))
                E(lambda v: v.tensor_tensor(out=u2, in0=u1, in1=ah, op=ALU.subtract))
                E(lambda v: v.tensor_tensor(out=t1, in0=u1, in1=u2, op=ALU.subtract))
                E(lambda v: v.tensor_tensor(out=t2, in0=ah, in1=t1, op=ALU.subtract))
                E(lambda v: v.tensor_scalar_mul(out=s1, in0=sh, scalar1=CST))
                E(lambda v: v.tensor_tensor(out=s2, in0=s1, in1=sh, op=ALU.subtract))
                E(lambda v: v.tensor_tensor(out=sc1, in0=s1, in1=s2, op=ALU.subtract))
                E(lambda v: v.tensor_tensor(out=sc2, in0=sh, in1=sc1, op=ALU.subtract))
                E(lambda v: v.tensor_scalar(out=e1, in0=t1, scalar1=sc1, scalar2=None,
                                            op0=ALU.mult))
                E(lambda v: v.tensor_scalar(out=u2, in0=ah, scalar1=sh, scalar2=None,
                                            op0=ALU.mult))
                E(lambda v: v.tensor_tensor(out=e1, in0=e1, in1=u2, op=ALU.subtract))
                E(lambda v: v.tensor_scalar(out=u1, in0=t1, scalar1=sc2, scalar2=None,
                                            op0=ALU.mult))
                E(lambda v: v.tensor_tensor(out=e1, in0=e1, in1=u1, op=ALU.add))
                E(lambda v: v.tensor_scalar(out=u1, in0=t2, scalar1=sc1, scalar2=None,
                                            op0=ALU.mult))
                E(lambda v: v.tensor_tensor(out=e1, in0=e1, in1=u1, op=ALU.add))
                E(lambda v: v.tensor_scalar(out=u1, in0=t2, scalar1=sc2, scalar2=None,
                                            op0=ALU.mult))
                E(lambda v: v.tensor_tensor(out=e1, in0=e1, in1=u1, op=ALU.add))
                E(lambda v: v.tensor_scalar(out=u1, in0=ah, scalar1=sl, scalar2=None,
                                            op0=ALU.mult))
                E(lambda v: v.tensor_tensor(out=e1, in0=e1, in1=u1, op=ALU.add))
                E(lambda v: v.tensor_scalar(out=u1, in0=al, scalar1=sh, scalar2=None,
                                            op0=ALU.mult))
                E(lambda v: v.tensor_tensor(out=e1, in0=e1, in1=u1, op=ALU.add))
                E(lambda v: v.tensor_tensor(out=u1, in0=u2, in1=e1, op=ALU.add))
                E(lambda v: v.tensor_tensor(out=t1, in0=u1, in1=u2, op=ALU.subtract))
                E(lambda v: v.tensor_tensor(out=dst_lo, in0=e1, in1=t1, op=ALU.subtract))
                E(lambda v: v.tensor_copy(out=dst_hi, in_=u1))

            def pinit(g):
                g.memset(tw[0:1, 228:229], float(R32_HI))
                return g.memset(tw[0:1, 229:230], float(R32_LO))
            p.e("gpsimd", pinit, reads=("rowbuf",), writes=("rowbuf",))
            P_hi = tw[0:1, 228:229]; P_lo = tw[0:1, 229:230]
            for n in (1, 2, 4, 8, 16):
                tw_mul(tw[0:1, n:2 * n], tw[0:1, 32 + n:32 + 2 * n],
                       tw[0:1, 0:n], tw[0:1, 32:32 + n], P_hi, P_lo)
                if n != 16:
                    tw_mul(tw[0:1, 234:235], tw[0:1, 235:236], P_hi, P_lo,
                           P_hi, P_lo)
                    p.e("vector", lambda v: v.tensor_copy(out=P_hi,
                                                           in_=tw[0:1, 234:235]),
                        reads=("rowbuf",), writes=("rowbuf",), hard=True)
                    p.e("vector", lambda v: v.tensor_copy(out=P_lo,
                                                          in_=tw[0:1, 235:236]),
                        reads=("rowbuf",), writes=("rowbuf",), hard=True)
            def arep(v):
                inst = None
                for tc in range(16):
                    inst = v.tensor_copy(out=tw[0:1, 1024 + tc * 32:1024 + (tc + 1) * 32],
                                         in_=tw[0:1, 0:32])
                return inst
            p.e("vector", arep, reads=("rowbuf",), writes=("rowbuf",), hard=True)
            if "tb2" in taps:
                p.e("scalar", lambda sc: sc.dma_start(out=taps["tb2"][0], in_=tw[:, :]),
                    dma="SO", reads=("rowbuf",))
            p.e("tensor", lambda t: t.matmul(pB[:, :], onesf[0:1, :], tw[0:1, 1024:1536],
                                             start=True, stop=True, skip_group_check=True),
                reads=("rowbuf", "init"), writes=("pb",))
            p.e("vector", lambda v: v.tensor_tensor(out=tmpQ[:, :], in0=tmpP[:, :],
                                                    in1=pB[:, :], op=ALU.mult),
                reads=("tmpP", "pb"), writes=("tmpQ",), hard=True)
            if "tb2" in taps:
                p.e("scalar", lambda sc: sc.dma_start(out=taps["tb2"][1][:, 0:512],
                                                      in_=tmpQ[:, :]),
                    dma="SO", reads=("tmpQ",))
            p.e("vector", lambda v: v.tensor_copy(out=tmpP[:, :], in_=tmpQ[:, :]),
                reads=("tmpQ",), writes=("tmpP",), hard=True)
            for k in range(21, -1, -1):
                M = float(np.float32(TWOPI_HI * np.float32(2.0 ** k)))
                p.e("vector", lambda v, M=M:
                    v.tensor_single_scalar(out=bcsb[:, :], in_=tmpP[:, :], scalar=M,
                                           op=ALU.is_ge),
                    reads=("tmpP", "bcsb"), writes=("bcsb",))
                p.e("vector", lambda v, M=M:
                    v.scalar_tensor_tensor(out=tmpP[:, :], in0=bcsb[:, :],
                                           scalar=-M, in1=tmpP[:, :],
                                           op0=ALU.mult, op1=ALU.add),
                    reads=("tmpP", "bcsb"), writes=("tmpP",))
            def FE(f):
                p.e("vector", f, reads=("tmpP", "tmpQ", "bcsb"),
                    writes=("tmpP", "tmpQ", "bcsb"))
            FE(lambda v: v.tensor_single_scalar(out=bcsb[:, :], in_=tmpP[:, :],
                                                scalar=float(PI_HI), op=ALU.is_gt))
            FE(lambda v: v.scalar_tensor_tensor(out=tmpP[:, :], in0=bcsb[:, :],
                                                scalar=-float(TWOPI_HI), in1=tmpP[:, :],
                                                op0=ALU.mult, op1=ALU.add))
            FE(lambda v: v.tensor_tensor(out=bcsb[:, :], in0=tmpQ[:, :], in1=tmpP[:, :],
                                         op=ALU.subtract))
            FE(lambda v: v.tensor_scalar_mul(out=bcsb[:, :], in0=bcsb[:, :],
                                             scalar1=float(INV2PI)))
            FE(lambda v: v.scalar_tensor_tensor(out=tmpP[:, :], in0=bcsb[:, :],
                                                scalar=-float(D_HI), in1=tmpP[:, :],
                                                op0=ALU.mult, op1=ALU.add))
            FE(lambda v: v.scalar_tensor_tensor(out=tmpP[:, :], in0=bcsb[:, :],
                                                scalar=-float(D_LO), in1=tmpP[:, :],
                                                op0=ALU.mult, op1=ALU.add))
            FE(lambda v: v.tensor_single_scalar(out=bcsb[:, :], in_=tmpP[:, :],
                                                scalar=float(PI_HI), op=ALU.is_gt))
            FE(lambda v: v.scalar_tensor_tensor(out=tmpP[:, :], in0=bcsb[:, :],
                                                scalar=-float(TWOPI_HI), in1=tmpP[:, :],
                                                op0=ALU.mult, op1=ALU.add))
            FE(lambda v: v.scalar_tensor_tensor(out=tmpP[:, :], in0=bcsb[:, :],
                                                scalar=-float(D_HI), in1=tmpP[:, :],
                                                op0=ALU.mult, op1=ALU.add))
            FE(lambda v: v.tensor_single_scalar(out=bcsb[:, :], in_=tmpP[:, :],
                                                scalar=-float(PI_HI), op=ALU.is_lt))
            FE(lambda v: v.scalar_tensor_tensor(out=tmpP[:, :], in0=bcsb[:, :],
                                                scalar=float(TWOPI_HI), in1=tmpP[:, :],
                                                op0=ALU.mult, op1=ALU.add))
            FE(lambda v: v.scalar_tensor_tensor(out=tmpP[:, :], in0=bcsb[:, :],
                                                scalar=float(D_HI), in1=tmpP[:, :],
                                                op0=ALU.mult, op1=ALU.add))
            from math import factorial
            CS = [float(np.float32((-1) ** k / factorial(2 * k + 1)))
                  for k in range(11)]
            CC = [float(np.float32((-1) ** k / factorial(2 * k)))
                  for k in range(12)]
            def PE_(f, rd, wr):
                p.e("vector", f, reads=rd, writes=wr)
            PE_(lambda v: v.tensor_tensor(out=bcsb[:, :], in0=tmpP[:, :],
                                          in1=tmpP[:, :], op=ALU.mult),
                ("tmpP", "bcsb"), ("bcsb",))
            PE_(lambda v: v.tensor_scalar(out=sin_tm[:, :], in0=bcsb[:, :],
                                          scalar1=CS[10], scalar2=CS[9],
                                          op0=ALU.mult, op1=ALU.add),
                ("bcsb", "sintm"), ("sintm",))
            for k in range(8, -1, -1):
                PE_(lambda v: v.tensor_tensor(out=sin_tm[:, :], in0=sin_tm[:, :],
                                              in1=bcsb[:, :], op=ALU.mult),
                    ("bcsb", "sintm"), ("sintm",))
                PE_(lambda v, k=k: v.tensor_scalar_add(out=sin_tm[:, :],
                                                       in0=sin_tm[:, :],
                                                       scalar1=CS[k]),
                    ("sintm",), ("sintm",))
            PE_(lambda v: v.tensor_tensor(out=sin_tm[:, :], in0=sin_tm[:, :],
                                          in1=tmpP[:, :], op=ALU.mult),
                ("sintm", "tmpP"), ("sintm",))
            PE_(lambda v: v.tensor_scalar(out=cos_tm[:, :], in0=bcsb[:, :],
                                          scalar1=CC[11], scalar2=CC[10],
                                          op0=ALU.mult, op1=ALU.add),
                ("bcsb", "costm"), ("costm",))
            for k in range(9, -1, -1):
                PE_(lambda v: v.tensor_tensor(out=cos_tm[:, :], in0=cos_tm[:, :],
                                              in1=bcsb[:, :], op=ALU.mult),
                    ("bcsb", "costm"), ("costm",))
                PE_(lambda v, k=k: v.tensor_scalar_add(out=cos_tm[:, :],
                                                       in0=cos_tm[:, :],
                                                       scalar1=CC[k]),
                    ("costm",), ("costm",))
            if "tab" in taps:
                def tp(sc):
                    return [sc.dma_start(out=taps["tab"][0], in_=cos_tm[:, :]),
                            sc.dma_start(out=taps["tab"][1], in_=sin_tm[:, :])]
                p.ed("scalar", tp, 2, "SO", reads=("costm", "sintm"))

        cur = {}

        def ln_load(seq, s):
            xin = cur["xin"]
            par = s % 2
            xin_ = xin[:, par * NCH * ST:(par + 1) * NCH * ST]
            rb = rowbuf[0:1, 0:1024]
            p.e("sync", lambda sy, seq=seq, s=s, rb=rb:
                sy.dma_start(out=rb[0:1, 0:1024], in_=statsb[seq][s:s + 1, :]),
                dma=f"LM{par}", reads=(f"stats{seq}_{s}", f"rstd{seq}"),
                writes=("rowbuf",))
            p.e("sync", lambda sy, seq=seq, s=s, xin_=xin_:
                sy.dma_start(out=xin_, in_=xscr[seq, s]),
                dma=f"LX{par}", reads=(f"xscr{seq}_{s}",), writes=(f"xin{par}",))

        def ln_rest(seq, s, store_ln):
            bufLN, xin = cur["bufLN"], cur["xin"]
            lnmr = cur.get("lnmr")
            par = s % 2
            xin_ = xin[:, par * NCH * ST:(par + 1) * NCH * ST]
            lnb = bufLN[:, par * NCH * ST:(par + 1) * NCH * ST]
            rb = rowbuf[0:1, 0:1024]
            def bc(t, rb=rb):
                t.matmul(pB[:, :], onesf[0:1, :], rb[0:1, 0:512], start=True,
                         stop=True, skip_group_check=True)
                return t.matmul(pD[:, :], onesf[0:1, :], rb[0:1, 512:1024],
                                start=True, stop=True, skip_group_check=True)
            p.e("tensor", bc, reads=("rowbuf", "init"), writes=("pb", "pd"))
            if lnmr is not None:
                mb = lnmr[:, 0:512]
                rs = lnmr[:, 512:1024]
                def cp(v):
                    v.tensor_copy(out=mb, in_=pB[:, :])
                    return v.tensor_copy(out=rs, in_=pD[:, :])
                p.e("vector", cp, reads=("pb", "pd"), writes=("lnmr",))
                mrr = ("lnmr",)
            else:
                mb, rs = pB[:, :], pD[:, :]
                mrr = ("pb", "pd")
            for dc in range(NCH):
                p.e("vector", lambda v, xin_=xin_, dc=dc, mb=mb:
                    v.tensor_tensor(out=tmpP[:, :],
                                    in0=xin_[:, dc * 512:(dc + 1) * 512],
                                    in1=mb, op=ALU.subtract),
                    reads=(f"xin{par}",) + mrr + ("tmpP",), writes=("tmpP",))
                p.e("vector", lambda v, lnb=lnb, dc=dc, rs=rs:
                    v.tensor_tensor(out=lnb[:, dc * 512:(dc + 1) * 512],
                                    in0=tmpP[:, :], in1=rs, op=ALU.mult),
                    reads=mrr + ("tmpP",), writes=(f"bufLN{par}",))
            if store_ln:
                p.e("scalar", lambda sc, s=s, lnb=lnb:
                    sc.dma_start(out=lnscr[s], in_=lnb),
                    dma="SL0" if par == 0 else "SL1",
                    reads=(f"bufLN{par}",), writes=(f"lnscr{s}",))

        def ln_build_stripe(seq, s, store_ln):
            ln_load(seq, s)
            ln_rest(seq, s, store_ln)

        def ln_build(seq, store_ln):
            for s in range(NST):
                ln_build_stripe(seq, s, store_ln)

        def ln_reload(s):
            bufLN = cur["bufLN"]
            par = s % 2
            lnb = bufLN[:, par * NCH * ST:(par + 1) * NCH * ST]
            p.e("sync", lambda sy, s=s, lnb=lnb: sy.dma_start(out=lnb, in_=lnscr[s]),
                dma=f"LL{par}", reads=(f"lnscr{s}",), writes=(f"bufLN{par}",))
            return lnb, par

        def fm_proj_group(wT_sb, src_ap, oc, extra_reads):
            def f(t):
                inst = None
                for dc in range(NCH):
                    inst = t.matmul(pD[:, :],
                                    wT_sb[:, dc * 1024 + oc * 128:dc * 1024 + oc * 128 + 128],
                                    src_ap(dc), start=(dc == 0), stop=(dc == NCH - 1),
                                    skip_group_check=True)
                return inst
            p.e("tensor", f, reads=extra_reads, writes=("pd",))

        def writeback(seq, s, produce, bias_col, tap=None, last_sub=False,
                      do_stats=True, xin_loaded=False, wparts=99, defer=None):
            xin, xout = cur["xin"], cur["xout"]
            par = s % 2
            xin_ = xin[:, par * NCH * ST:(par + 1) * NCH * ST]
            if not xin_loaded:
                p.e("sync", lambda sy, seq=seq, s=s, xin_=xin_:
                    sy.dma_start(out=xin_, in_=xscr[seq, s]),
                    dma=f"LX{par}", reads=(f"xscr{seq}_{s}",), writes=(f"xin{par}",))

            def emit_oc(oc):
                if wparts < 1:
                    return
                produce(oc)
                if wparts < 2:
                    return
                dst = xout[:, par * NCH * ST + oc * 512:par * NCH * ST + (oc + 1) * 512]
                src = xin_[:, oc * 512:(oc + 1) * 512]
                if bias_col is not None:
                    p.e("vector", lambda v, dst=dst, src=src, oc=oc:
                        v.scalar_tensor_tensor(out=dst, in0=pD[:, :],
                                               scalar=bias_col[:, oc:oc + 1], in1=src,
                                               op0=ALU.add, op1=ALU.add),
                        reads=("pd", f"xin{par}", "bias"), writes=(f"xin{par}",))
                else:
                    p.e("vector", lambda v, dst=dst, src=src:
                        v.tensor_tensor(out=pD[:, :] if False else dst, in0=pD[:, :],
                                        in1=src, op=ALU.add),
                        reads=("pd", f"xin{par}"), writes=(f"xin{par}",))
                if do_stats and wparts >= 3:
                    if oc >= 1:
                        stats_mm(oc - 1)
                    stats_dve(seq, s, dst, oc, par, f"xin{par}")

            def emit_stats_tail():
                if do_stats and wparts >= 3:
                    stats_mm(NCH - 1)
                    stats_row_store(seq, s)

            if defer is not None:
                for oc in range(NCH):
                    defer.append(lambda oc=oc: emit_oc(oc))
                defer.append(emit_stats_tail)
            else:
                for oc in range(NCH):
                    emit_oc(oc)
                emit_stats_tail()
            if wparts < 4:
                return
            if not last_sub:
                xo = xout[:, par * NCH * ST:(par + 1) * NCH * ST]

                def emit_store():
                    p.e("scalar", lambda sc, seq=seq, s=s, xo=xo:
                        sc.dma_start(out=xscr[seq, s], in_=xo),
                        dma="SX0" if par == 0 else "SX1",
                        reads=(f"xin{par}",), writes=(f"xscr{seq}_{s}",))
                    if tap is not None:
                        p.e("scalar", lambda sc, seq=seq, s=s, xo=xo:
                            sc.dma_start(out=taps[tap][seq, s], in_=xo),
                            dma="SO", reads=(f"xin{par}",), writes=())
                if defer is not None:
                    defer.append(emit_store)
                else:
                    emit_store()
            else:
                outtm = cur["outtm"]
                for tb in range(4):
                    opar = tb % 2
                    for dc in range(NCH):
                        bank = dc % 2
                        srcc = xout[:, par * NCH * ST + dc * 512 + tb * 128:
                                    par * NCH * ST + dc * 512 + tb * 128 + 128]
                        p.e("tensor", lambda t, srcc=srcc, bank=bank:
                            t.transpose(pS[bank][:, 0:128], srcc, identf[:, :]),
                            reads=(f"xin{par}", "init"), writes=(f"ps{bank}",))
                        p.e("vector", lambda v, dc=dc, bank=bank, opar=opar:
                            v.tensor_copy(out=outtm[:, opar * 1024 + dc * 128:
                                                    opar * 1024 + dc * 128 + 128],
                                          in_=pS[bank][:, 0:128]),
                            reads=(f"ps{bank}",), writes=(f"outtm{opar}",))
                    p.e("scalar", lambda sc, seq=seq, s=s, tb=tb, opar=opar:
                        sc.dma_start(out=out_t[seq, s * 512 + tb * 128:
                                               s * 512 + (tb + 1) * 128, :],
                                     in_=outtm[:, opar * 1024:(opar + 1) * 1024]),
                        dma=f"SP{opar}", reads=(f"outtm{opar}",), writes=())

        def load_w_fm(w_ap, col0, ncols):
            def f(g):
                return [g.dma_start(out=wT[:, dc * 1024:dc * 1024 + ncols],
                                    in_=w_ap[dc * 128:(dc + 1) * 128, col0:col0 + ncols])
                        for dc in range(NCH)]
            p.ed("gpsimd", f, NCH, "GW0", writes=("wT",))

        def attn_a_phase():
          with contextlib.ExitStack() as aes:
              asb = lambda name, shape, dt=BF16: aes.enter_context(
                  nc.sbuf_tensor(name, shape, dt))
              bufQ = asb("bufQ", [128, NCH * S])
              bufK = asb("bufK", [128, NCH * S])
              bufV = asb("bufV", [128, 16 * 1040])
              bufLN = asb("bufLN", [128, 2 * NCH * ST])
              bufPT = bufLN
              bufY = asb("bufY", [128, NCH * ST])
              xin = asb("xinA", [128, 2 * NCH * ST], F32)
              qtm = asb("qtm", [128, 2 * 512])
              cos_tm = asb("cos_tm", [128, 512], F32)
              sin_tm = asb("sin_tm", [128, 512], F32)
              wnat = asb("wnatA", [128, 2 * 1024], BF16)
              ystage = qtm
              cur.update(bufLN=bufLN, xin=xin, xout=xin)
              build_tables(cos_tm, sin_tm)
              p.e("gpsimd", lambda g: g.memset(
                  AP(bufV, 64, [[16 * 1040, 128], [1040, 16], [65, 16]]), 1.0),
                  writes=("vones",))
              def prep_wT(w_ap, slot, cached):
                  if cached:
                      p.e("sync", lambda sy: sy.dma_start(out=wT[:, :], in_=wscr[slot]),
                          dma="LW", reads=(f"wscr{slot}",), writes=("wT",))
                      return
                  for oc in range(NCH):
                      par = oc % 2
                      wn = wnat[:, par * 1024:(par + 1) * 1024]
                      p.e("gpsimd", lambda g, oc=oc, wn=wn:
                          g.dma_start(out=wn, in_=w_ap[oc * 128:(oc + 1) * 128, :]),
                          dma=f"GW{par}", writes=(f"wnat{par}",))
                      for dc in range(NCH):
                          bank = dc % 2
                          p.e("tensor", lambda t, wn=wn, dc=dc, bank=bank:
                              t.transpose(pSb[:, bank * 1024:bank * 1024 + 128],
                                          wn[:, dc * 128:(dc + 1) * 128], identb[:, :]),
                              reads=(f"wnat{par}", "init"), writes=(f"ps{bank}",))
                          p.e("vector", lambda v, dc=dc, oc=oc, bank=bank:
                              v.tensor_copy(out=wT[:, dc * 1024 + oc * 128:
                                                   dc * 1024 + oc * 128 + 128],
                                            in_=pSb[:, bank * 1024:bank * 1024 + 128]),
                              reads=(f"ps{bank}",), writes=("wT",))
                  p.e("scalar", lambda sc, slot=slot: sc.dma_start(out=wscr[slot], in_=wT[:, :]),
                      dma=f"SW{slot}", reads=("wT",), writes=(f"wscr{slot}",))



              def qkv_tm_pass(seq, dst, rope, dst_res):
                  pend = [None]
                  for s in range(NST):
                      lnb, par = ln_reload(s)
                      for ti in range(4):
                          tc = s * 4 + ti
                          for ns in range(2):
                              bank = 2 + (tc * 2 + ns) % 2
                              def mm(t, lnb=lnb, ti=ti, ns=ns, bank=bank):
                                  inst = None
                                  for dc in range(NCH):
                                      inst = t.matmul(
                                          pS[bank][:, :],
                                          lnb[:, dc * 512 + ti * 128:dc * 512 + ti * 128 + 128],
                                          wT[:, dc * 1024 + ns * 512:dc * 1024 + (ns + 1) * 512],
                                          start=(dc == 0), stop=(dc == NCH - 1),
                                          skip_group_check=True)
                                  return inst
                              p.e("tensor", mm, reads=(f"bufLN{par}", "wT"),
                                  writes=(f"ps{bank}",))
                              if rope:
                                  if pend[0] is not None:
                                      pend[0]()
                                      pend[0] = None
                                  qpar = (tc * 2 + ns) % 2
                                  qt = qtm[:, qpar * 512:(qpar + 1) * 512]
                                  po = bank * 512
                                  tr_i = AP(pS4, po, [[2048, 128], [64, 8], [2, 32]])
                                  ti_i = AP(pS4, po + 1, [[2048, 128], [64, 8], [2, 32]])
                                  cs_ = AP(cos_tm, tc * 32, [[512, 128], [0, 8], [1, 32]])
                                  sn_ = AP(sin_tm, tc * 32, [[512, 128], [0, 8], [1, 32]])
                                  qo = qpar * 512
                                  tr_o = AP(qtm, qo, [[1024, 128], [64, 8], [2, 32]])
                                  ti_o = AP(qtm, qo + 1, [[1024, 128], [64, 8], [2, 32]])
                                  tP = AP(tmpP, 0, [[512, 128], [32, 8], [1, 32]])
                                  tQ = AP(tmpQ, 0, [[512, 128], [32, 8], [1, 32]])
                                  p.e("vector", lambda v, tr_i=tr_i, cs_=cs_, tP=tP:
                                      v.tensor_tensor(out=tP, in0=tr_i, in1=cs_,
                                                      op=ALU.mult),
                                      reads=(f"ps{bank}", "costm", "tmpP"),
                                      writes=("tmpP",))
                                  p.e("vector", lambda v, ti_i=ti_i, sn_=sn_, tQ=tQ:
                                      v.tensor_tensor(out=tQ, in0=ti_i, in1=sn_,
                                                      op=ALU.mult),
                                      reads=(f"ps{bank}", "sintm", "tmpQ"),
                                      writes=("tmpQ",))
                                  p.e("vector", lambda v, tr_o=tr_o, tP=tP, tQ=tQ:
                                      v.tensor_tensor(out=tr_o, in0=tP, in1=tQ,
                                                      op=ALU.subtract),
                                      reads=("tmpP", "tmpQ", f"qtm{qpar}"),
                                      writes=(f"qtm{qpar}",))
                                  p.e("vector", lambda v, tr_i=tr_i, sn_=sn_, tP=tP:
                                      v.tensor_tensor(out=tP, in0=tr_i, in1=sn_,
                                                      op=ALU.mult),
                                      reads=(f"ps{bank}", "sintm", "tmpP"),
                                      writes=("tmpP",))
                                  p.e("vector", lambda v, ti_i=ti_i, cs_=cs_, tQ=tQ:
                                      v.tensor_tensor(out=tQ, in0=ti_i, in1=cs_,
                                                      op=ALU.mult),
                                      reads=(f"ps{bank}", "costm", "tmpQ"),
                                      writes=("tmpQ",))
                                  p.e("vector", lambda v, ti_o=ti_o, tP=tP, tQ=tQ:
                                      v.tensor_tensor(out=ti_o, in0=tP, in1=tQ,
                                                      op=ALU.add),
                                      reads=("tmpP", "tmpQ", f"qtm{qpar}"),
                                      writes=(f"qtm{qpar}",))
                                  def tp_unit(qt=qt, qpar=qpar, tc=tc, ns=ns):
                                      for j in range(4):
                                          tb = j % 2
                                          p.e("tensor", lambda t, qt=qt, j=j, tb=tb:
                                              t.transpose(pY2b[:, tb * 1024:tb * 1024 + 128],
                                                          qt[:, j * 128:(j + 1) * 128],
                                                          identb[:, :]),
                                              reads=(f"qtm{qpar}", "init"),
                                              writes=(f"py{tb}",))
                                          p.e("vector", lambda v, dst=dst, tc=tc, ns=ns, j=j, tb=tb:
                                              v.tensor_copy(
                                                  out=dst[:, (ns * 4 + j) * S + tc * 128:
                                                          (ns * 4 + j) * S + tc * 128 + 128],
                                                  in_=pY2b[:, tb * 1024:tb * 1024 + 128]),
                                              reads=(f"py{tb}",), writes=(dst_res,))
                                  pend[0] = tp_unit
                              else:
                                  def vev(v, tc=tc, ns=ns, bank=bank):
                                      o_ap = AP(bufV, tc * 1040 + ns * 8 * 65,
                                                [[16 * 1040, 128], [65, 8], [1, 64]])
                                      return v.tensor_copy(out=o_ap, in_=pS[bank][:, :])
                                  p.e("vector", vev, reads=(f"ps{bank}",),
                                      writes=("bufV",))
                  if pend[0] is not None:
                      pend[0]()
                      pend[0] = None

              def attn_a_attention(seq):
                  NSLOT = H * 8

                  def smm_mms(t, s, i):
                      h, g = divmod(i, 8)
                      hb = (h % 2) * 64
                      ch = h // 2
                      b0 = 2 * (g % 2)
                      inst = None
                      for j in range(2):
                          tkc = 2 * g + j
                          inst = t.matmul(
                              pS[b0 + j][:, :],
                              bufK[hb:hb + 64, ch * S + tkc * 128:
                                   ch * S + (tkc + 1) * 128],
                              bufQ[hb:hb + 64, ch * S + s * 512:
                                   ch * S + (s + 1) * 512],
                              start=True, stop=True, tile_position=(hb, 0),
                              skip_group_check=True)
                      return inst

                  def avm_mms(t, i):
                      h, g = divmod(i, 8)
                      ypar = h % 2
                      inst = None
                      for j in range(2):
                          tkc = 2 * g + j
                          inst = t.matmul(
                              pY[ypar][0:65, :],
                              bufV[:, tkc * 1040 + h * 65:
                                   tkc * 1040 + h * 65 + 65],
                              bufPT[:, g * 1024 + j * 512:
                                    g * 1024 + (j + 1) * 512],
                              start=(tkc == 0), stop=(tkc == 15),
                              skip_group_check=True)
                      return inst

                  def emit_slot(s, i):
                      hi, gi = divmod(i, 8) if i < NSLOT else (None, None)
                      reads = []
                      writes = []
                      if i < NSLOT:
                          b0 = 2 * (gi % 2)
                          reads += ["bufQK"]
                          writes += [f"ps{b0}", f"ps{b0 + 1}"]
                      ia = i - 2
                      if ia >= 0:
                          ha, ga = divmod(ia, 8)
                          reads += [f"bufPT{ga}", "bufV", "vones"]
                          if ia == NSLOT - 1:
                              reads += ["bufLN0", "bufLN1"]
                          if ga in (0, 7):
                              writes += [f"py{ha % 2}"]

                      def tf(t, s=s, i=i, ia=ia):
                          inst = None
                          if i < NSLOT:
                              inst = smm_mms(t, s, i)
                          if ia >= 0:
                              inst = avm_mms(t, ia)
                          return inst
                      p.e("tensor", tf, reads=tuple(reads), writes=tuple(writes))
                      if i < NSLOT:
                          b0 = 2 * (gi % 2)
                          exw = (f"bufPT{gi}",) + (("bufLN0", "bufLN1") if i == 0 else ())
                          p.e("scalar", lambda sc, g=gi, b0=b0:
                              sc.activation(out=bufPT[:, g * 1024:(g + 1) * 1024],
                                            in_=pS4[:, (g % 2) * 1024:
                                                    (g % 2) * 1024 + 1024],
                                            func=AF.Exp, scale=0.125),
                              reads=(f"ps{b0}", f"ps{b0 + 1}", "init"),
                              writes=exw)
                      if ia >= 0 and ia % 8 == 7:
                          emit_tail_recip(ia // 8)

                  def emit_tail_recip(h):
                          ypar = h % 2
                          p.e("scalar", lambda sc, ypar=ypar:
                              sc.activation(out=recipb[64:65, 0:512],
                                            in_=pY[ypar][64:65, :], func=AF.Ln),
                              reads=(f"py{ypar}", "init", "rowbuf"),
                              writes=("rowbuf",))
                          p.e("scalar", lambda sc:
                              sc.activation(out=recipb[64:65, 512:1024],
                                            in_=recipb[64:65, 0:512],
                                            func=AF.Exp, scale=-1.0),
                              reads=("rowbuf",), writes=("rowbuf",))

                  def emit_tail_b(h):
                          ypar = h % 2
                          p.e("tensor", lambda t:
                              t.matmul(pB[0:64, :], onesf[64:65, 0:64],
                                       recipb[64:65, 512:1024], start=True, stop=True,
                                       skip_group_check=True),
                              reads=("rowbuf",), writes=("pb",))
                          p.e("vector", lambda v: v.tensor_copy(out=bcsb[0:64, :],
                                                                in_=pB[0:64, :]),
                              reads=("pb",), writes=("bcs",))
                          if h % 2 == 0:
                              p.e("vector", lambda v, h=h, ypar=ypar:
                                  v.tensor_tensor(out=bufY[0:64, (h // 2) * 512:
                                                           (h // 2) * 512 + 512],
                                                  in0=pY[ypar][0:64, :], in1=bcsb[0:64, :],
                                                  op=ALU.mult),
                                  reads=("bcs", f"py{ypar}"),
                                  writes=("bufY",))
                          else:
                              yspar = (h // 2) % 2
                              p.e("vector", lambda v, ypar=ypar, yspar=yspar:
                                  v.tensor_tensor(out=ystage[0:64, yspar * 512:
                                                             (yspar + 1) * 512],
                                                  in0=pY[ypar][0:64, :], in1=bcsb[0:64, :],
                                                  op=ALU.mult),
                                  reads=("bcs", f"py{ypar}"),
                                  writes=(f"qtm{yspar}",))
                              p.e("scalar", lambda sc, h=h, yspar=yspar:
                                  sc.dma_start(out=bufY[64:128, (h // 2) * 512:
                                                        (h // 2) * 512 + 512],
                                               in_=ystage[0:64, yspar * 512:
                                                          (yspar + 1) * 512]),
                                  dma=f"SY{yspar}", reads=(f"qtm{yspar}",),
                                  writes=("bufY",))
                  deferq = []
                  for s in range(NST):
                      for i in range(NSLOT + 2):
                          emit_slot(s, i)
                          if deferq and i >= 1:
                              deferq.pop(0)()
                          if i >= 13 and (i - 13) % 8 == 0:
                              emit_tail_b((i - 13) // 8)
                      emit_tail_b(H - 1)
                      while deferq:
                          deferq.pop(0)()
                      def produce(oc):
                          fm_proj_group(wT, lambda dc: bufY[:, dc * 512:(dc + 1) * 512],
                                        oc, ("wT", "bufY"))
                      writeback(seq, s, produce, None,
                                tap=("x1" if "x1" in taps else None),
                                defer=(deferq if s < NST - 1 else None))

              for seq in range(SEQ):
                  ln_build(seq, store_ln=True)
                  prep_wT(wq, 0, cached=(seq > 0))
                  qkv_tm_pass(seq, bufQ, True, "bufQK")
                  prep_wT(wk, 1, cached=(seq > 0))
                  qkv_tm_pass(seq, bufK, True, "bufQK")
                  if "qk" in taps and seq == 0:
                      def qk_tap(g):
                          return [g.dma_start(out=taps["qk"][0], in_=bufQ[:, :]),
                                  g.dma_start(out=taps["qk"][1], in_=bufK[:, :])]
                      p.ed("gpsimd", qk_tap, 2, "GW0", reads=("bufQK",))
                  prep_wT(wv, 2, cached=(seq > 0))
                  qkv_tm_pass(seq, bufV, False, "bufV")
                  prep_wT(wo, 3, cached=(seq > 0))
                  attn_a_attention(seq)
                  stats_finalize(seq)
              p.barrier()

        if PHASES >= 2:
            attn_a_phase()

        mTp = {}

        def ffn_sublayer(last):
            with contextlib.ExitStack() as fes:
                fsb = lambda name, shape, dt=F32: fes.enter_context(
                    nc.sbuf_tensor(name, shape, dt))
                sfx = "2" if last else "1"
                mT = fsb("mT" + sfx, [128, NCH * 1024], BF16)
                if last:
                    p.e("sync", lambda sy: sy.dma_start(out=mT[:, :],
                                                        in_=mtscr[:, :]),
                        dma="LL0", writes=("mT",))
                else:
                    wacc = fsb("wacc" + sfx, [128, 2 * 2048])
                    stgA = fsb("stgA" + sfx, [128, 2 * 512])
                    stgB = fsb("stgB" + sfx, [128, 2 * 512])
                bufLN = fsb("bufLNf" + sfx, [128, 2 * NCH * ST], BF16)
                xin = fsb("xinF" + sfx, [128, 2 * NCH * ST])
                outtm = fsb("outtm", [128, 2 * 1024]) if last else None
                lnmr = fsb("lnmr" + sfx, [128, 1024])
                cur.update(bufLN=bufLN, xin=xin, xout=xin, outtm=outtm, lnmr=lnmr)

                for var, (Ae, Be) in enumerate((() if last else
                                                ((aexp, bexp), (aexpi, bexpi)))):
                    for kc in range(4):
                        for c in range(4):
                            par = c % 2
                            def ld(g, Ae=Ae, Be=Be, c=c, kc=kc, par=par):
                                return [
                                    g.dma_start(out=stgA[:, par * 512:(par + 1) * 512],
                                                in_=Ae[c, kc * 128:(kc + 1) * 128, :]),
                                    g.dma_start(out=stgB[:, par * 512:(par + 1) * 512],
                                                in_=Be[c, kc * 128:(kc + 1) * 128, :])]
                            p.ed("gpsimd", ld, 2, f"GS{par}", writes=(f"stg{par}",))
                            wslice = wacc[:, var * 2048 + kc * 512:
                                          var * 2048 + (kc + 1) * 512]
                            if c == 0:
                                p.e("vector", lambda v, wslice=wslice, par=par:
                                    v.tensor_tensor(out=wslice,
                                                    in0=stgA[:, par * 512:(par + 1) * 512],
                                                    in1=stgB[:, par * 512:(par + 1) * 512],
                                                    op=ALU.mult),
                                    reads=(f"stg{par}",),
                                    writes=(f"wacc{var}_{kc}",))
                            else:
                                p.e("vector", lambda v, par=par:
                                    v.tensor_tensor(out=tmpP[:, :],
                                                    in0=stgA[:, par * 512:(par + 1) * 512],
                                                    in1=stgB[:, par * 512:(par + 1) * 512],
                                                    op=ALU.mult),
                                    reads=(f"stg{par}", "tmpP"), writes=("tmpP",))
                                p.e("vector", lambda v, wslice=wslice:
                                    v.tensor_tensor(out=wslice, in0=wslice,
                                                    in1=tmpP[:, :], op=ALU.add),
                                    reads=("tmpP", f"wacc{var}_{kc}"),
                                    writes=(f"wacc{var}_{kc}",))
                for kc in (() if last else range(4)):
                    wr = wacc[:, kc * 512:(kc + 1) * 512]
                    wi = wacc[:, 2048 + kc * 512:2048 + (kc + 1) * 512]
                    def asm(v, kc=kc, wr=wr, wi=wi):
                        v.tensor_copy(out=mT[:, kc * 1024:kc * 1024 + 512], in_=wr)
                        v.tensor_copy(out=mT[:, kc * 1024 + 512:kc * 1024 + 1024], in_=wi)
                        v.tensor_scalar_mul(out=mT[:, (4 + kc) * 1024:
                                                   (4 + kc) * 1024 + 512],
                                            in0=wi, scalar1=-1.0)
                        return v.tensor_copy(out=mT[:, (4 + kc) * 1024 + 512:
                                                    (4 + kc) * 1024 + 1024], in_=wr)
                    p.e("vector", asm, reads=(f"wacc0_{kc}", f"wacc1_{kc}"),
                        writes=("mT",))
                if not last:
                    p.e("scalar", lambda sc: sc.dma_start(out=mtscr[:, :],
                                                          in_=mT[:, :]),
                        dma="SL0", reads=("mT",), writes=())

                for seq in range(SEQ):
                    if FFN_PARTS < 2:
                        break
                    ln_build_stripe(seq, 0, store_ln=False)
                    for s in range(NST):
                        par = s % 2
                        if s + 1 < NST:
                            ln_build_stripe(seq, s + 1, store_ln=False)
                        if FFN_PARTS < 3:
                            continue
                        def produce(oc, par=par):
                            fm_proj_group(
                                mT,
                                lambda dc: bufLN[:, par * NCH * ST + dc * 512:
                                                 par * NCH * ST + (dc + 1) * 512],
                                oc, ("mT", f"bufLN{par}"))
                        writeback(seq, s, produce, fbias,
                                  tap=("x2" if ("x2" in taps and not last) else None),
                                  last_sub=last, do_stats=not last, xin_loaded=True)
                    if not last:
                        stats_finalize(seq)
            p.barrier()

        if PHASES >= 3:
            ffn_sublayer(last=False)

        def attn_b_phase():
          with contextlib.ExitStack() as bes:
              bsb = lambda name, shape, dt=BF16: bes.enter_context(
                  nc.sbuf_tensor(name, shape, dt))
              bufQ = bsb("bufQb", [128, NCH * S])
              bufK = bsb("bufKb", [128, NCH * S])
              bufV = bsb("bufVb", [128, 16 * 1024])
              bufLN = bsb("bufLNb", [128, 2 * NCH * ST])
              bufPT = bufLN
              bufY = bsb("bufYb", [128, NCH * ST])
              xin = bsb("xinB", [128, 2 * NCH * ST], F32)
              vbias = bsb("vbias", [128, 512], F32)
              vbias2 = tmpQ
              dmat = bsb("dmat", [128, 128], F32)
              cur.update(bufLN=bufLN, xin=xin, xout=xin, lnmr=None)
              def vb_ld(sy):
                  return [sy.dma_start(out=vbias[0:1, :],
                                       in_=bqkv[0:1, 2048:2560]),
                          sy.dma_start(out=vbias2[0:1, :],
                                       in_=bqkv[0:1, 2560:3072])]
              p.ed("sync", vb_ld, 2, "LW", reads=("bias",), writes=("vbiasrow",))

              def decay_init(g):
                  return g.iota(dmat[:, :], [[1, 128]], base=0, channel_multiplier=-1,
                                allow_small_or_imprecise_dtypes=True)
              p.e("gpsimd", decay_init, writes=("dmat",))
              p.e("scalar", lambda sc: sc.activation(out=lngc[0:1, :], in_=lngc[0:1, :],
                                                     func=AF.Ln),
                  reads=("lngrow", "init"), writes=("lngrow",))
              p.e("tensor", lambda t: t.matmul(pB[:, 0:1], onesf[0:1, :], lngc[0:1, 0:1],
                                               start=True, stop=True,
                                               skip_group_check=True),
                  reads=("lngrow",), writes=("pb",))
              p.e("vector", lambda v: v.tensor_copy(out=lngc[:, :], in_=pB[:, 0:1]),
                  reads=("pb",), writes=("lngcol",))
              p.e("scalar", lambda sc: sc.activation(out=dmat[:, :], in_=dmat[:, :],
                                                     func=AF.Exp, scale=lngc[:, 0:1]),
                  reads=("dmat", "lngcol"), writes=("dmat",))
              p.e("gpsimd", lambda g: g.affine_select(dmat[:, :], dmat[:, :], [[1, 128]],
                                                      ALU.is_ge, 0.0, base=0,
                                                      channel_multiplier=-1),
                  reads=("dmat",), writes=("dmat",))
              def vb_bc(t):
                  t.matmul(pB[:, :], onesf[0:1, :], vbias[0:1, 0:512], start=True,
                           stop=True, skip_group_check=True)
                  return t.matmul(pD[:, :], onesf[0:1, :], vbias2[0:1, 0:512],
                                  start=True, stop=True, skip_group_check=True)
              p.e("tensor", vb_bc, reads=("vbiasrow", "init"), writes=("pb", "pd"))
              def vb_cp(v):
                  v.tensor_copy(out=vbias[:, 0:512], in_=pB[:, :])
                  return v.tensor_copy(out=vbias2[:, 0:512], in_=pD[:, :])
              p.e("vector", vb_cp, reads=("pb", "pd", "vbiasrow"),
                  writes=("vbias",))
              def qk_fm_pass(seq, dst, bias0):
                  for s in range(NST):
                      lnb, par = ln_reload(s)
                      for oc in range(NCH):
                          bank = 2 + oc % 2
                          def mm(t, lnb=lnb, oc=oc, bank=bank):
                              inst = None
                              for dc in range(NCH):
                                  inst = t.matmul(
                                      pS[bank][:, :],
                                      wT[:, dc * 1024 + oc * 128:dc * 1024 + oc * 128 + 128],
                                      lnb[:, dc * 512:(dc + 1) * 512],
                                      start=(dc == 0), stop=(dc == NCH - 1),
                                      skip_group_check=True)
                              return inst
                          p.e("tensor", mm, reads=(f"bufLN{par}", "wT"),
                              writes=(f"ps{bank}",))
                          p.e("vector", lambda v, dst=dst, s=s, oc=oc, bank=bank,
                              bias0=bias0:
                              v.tensor_scalar(out=dst[:, oc * S + s * 512:
                                                      oc * S + (s + 1) * 512],
                                              in0=pS[bank][:, :],
                                              scalar1=qkbias[:, bias0 + oc:bias0 + oc + 1],
                                              scalar2=None, op0=ALU.add),
                              reads=(f"ps{bank}", "bias"),
                              writes=("bufQK",))

              def vb_tm_pass(seq):
                  for s in range(NST):
                      lnb, par = ln_reload(s)
                      for ti in range(4):
                          tc = s * 4 + ti
                          for ns in range(2):
                              bank = 2 + (tc * 2 + ns) % 2
                              def mm(t, lnb=lnb, ti=ti, ns=ns, bank=bank):
                                  inst = None
                                  for dc in range(NCH):
                                      inst = t.matmul(
                                          pS[bank][:, :],
                                          lnb[:, dc * 512 + ti * 128:
                                              dc * 512 + ti * 128 + 128],
                                          wT[:, dc * 1024 + ns * 512:
                                             dc * 1024 + (ns + 1) * 512],
                                          start=(dc == 0), stop=(dc == NCH - 1),
                                          skip_group_check=True)
                                  return inst
                              p.e("tensor", mm, reads=(f"bufLN{par}", "wT"),
                                  writes=(f"ps{bank}",))
                              def vev(v, tc=tc, ns=ns, bank=bank):
                                  return v.scalar_tensor_tensor(
                                      out=bufV[:, tc * 1024 + ns * 512:
                                               tc * 1024 + (ns + 1) * 512],
                                      in0=pS[bank][:, :], scalar=1.0,
                                      in1=(vbias if ns == 0 else vbias2)[:, 0:512],
                                      op0=ALU.mult, op1=ALU.add)
                              p.e("vector", vev, reads=(f"ps{bank}", "vbias"),
                                  writes=("bufV",))

              def attn_b_attention(seq):
                  for s in range(NST):
                      if B_ATT < 1:
                          break
                      for tkc in range(16):
                          bank = tkc % 2
                          def mm(t, s=s, tkc=tkc, bank=bank):
                              inst = None
                              for dc in range(NCH):
                                  inst = t.matmul(
                                      pS[bank][:, :],
                                      bufK[:, dc * S + tkc * 128:dc * S + (tkc + 1) * 128],
                                      bufQ[:, dc * S + s * 512:dc * S + (s + 1) * 512],
                                      start=(dc == 0), stop=(dc == NCH - 1),
                                      skip_group_check=True)
                              return inst
                          p.e("tensor", mm, reads=("bufQK",), writes=(f"ps{bank}",))
                          if B_ATT >= 2 and s * 4 <= tkc < s * 4 + 4:
                              off = (tkc - s * 4) * 128
                              p.e("vector", lambda v, bank=bank, off=off:
                                  v.tensor_tensor(out=pS[bank][:, off:off + 128],
                                                  in0=pS[bank][:, off:off + 128],
                                                  in1=dmat[:, :], op=ALU.mult),
                                  reads=(f"ps{bank}", "dmat"), writes=(f"ps{bank}",))
                          p.e("scalar", lambda sc, tkc=tkc, bank=bank:
                              sc.activation(out=bufPT[:, tkc * 512:(tkc + 1) * 512],
                                            in_=pS[bank][:, :], func=AF.Exp,
                                            scale=0.03125),
                              reads=(f"ps{bank}", "init"),
                              writes=(f"bufLN{tkc // 8}",))
                      if B_ATT < 3:
                          continue
                      def den(t):
                          inst = None
                          for tkc in range(16):
                              inst = t.matmul(pY[0][0:1, :], onesb1[:, :],
                                              bufPT[:, tkc * 512:(tkc + 1) * 512],
                                              start=(tkc == 0), stop=(tkc == 15),
                                              skip_group_check=True)
                          return inst
                      p.e("tensor", den, reads=("bufLN0", "bufLN1"),
                          writes=("py0",))
                      p.e("scalar", lambda sc:
                          sc.activation(out=recipb[0:1, 0:512], in_=pY[0][0:1, :],
                                        func=AF.Ln),
                          reads=("py0", "init", "rowbuf"), writes=("rowbuf",))
                      p.e("scalar", lambda sc:
                          sc.activation(out=recipb[0:1, 512:1024],
                                        in_=recipb[0:1, 0:512], func=AF.Exp,
                                        scale=-1.0),
                          reads=("rowbuf",), writes=("rowbuf",))
                      p.e("tensor", lambda t: t.matmul(pB[:, :], onesf[0:1, :],
                                                       recipb[0:1, 512:1024], start=True,
                                                       stop=True, skip_group_check=True),
                          reads=("rowbuf",), writes=("pb",))
                      p.e("vector", lambda v: v.tensor_copy(out=bcsb[:, :], in_=pB[:, :]),
                          reads=("pb",), writes=("bcs",))
                      if B_ATT < 4:
                          continue
                      for oc in range(NCH):
                          ypar = oc % 2
                          def avm(t, oc=oc, ypar=ypar):
                              inst = None
                              for tkc in range(16):
                                  inst = t.matmul(
                                      pY[ypar][:, :],
                                      bufV[:, tkc * 1024 + oc * 128:
                                           tkc * 1024 + (oc + 1) * 128],
                                      bufPT[:, tkc * 512:(tkc + 1) * 512],
                                      start=(tkc == 0), stop=(tkc == 15),
                                      skip_group_check=True)
                              return inst
                          p.e("tensor", avm,
                              reads=("bufLN0", "bufLN1", "bufV"),
                              writes=(f"py{ypar}",))
                          p.e("vector", lambda v, oc=oc, ypar=ypar:
                              v.tensor_tensor(out=bufY[:, oc * 512:(oc + 1) * 512],
                                              in0=pY[ypar][:, :], in1=bcsb[:, :],
                                              op=ALU.mult),
                              reads=("bcs", f"py{ypar}"),
                              writes=("bufY",))
                      if B_ATT < 5:
                          continue
                      def produce(oc):
                          fm_proj_group(wT, lambda dc: bufY[:, dc * 512:(dc + 1) * 512],
                                        oc, ("wT", "bufY"))
                      writeback(seq, s, produce, pbias,
                                tap=("x3" if "x3" in taps else None), wparts=W_PARTS)

              for seq in range(SEQ):
                  if B_PARTS < 2:
                      break
                  ln_build(seq, store_ln=True)
                  if B_PARTS >= 3:
                      load_w_fm(wqkv, 0, 1024)
                      qk_fm_pass(seq, bufQ, 0)
                      load_w_fm(wqkv, 1024, 1024)
                      qk_fm_pass(seq, bufK, 8)
                  if B_PARTS >= 4:
                      load_w_fm(wqkv, 2048, 1024)
                      vb_tm_pass(seq)
                  if B_PARTS >= 5:
                      if B_ATT != -2:
                          load_w_fm(wproj, 0, 1024)
                      attn_b_attention(seq)
                      if B_ATT >= 0:
                          stats_finalize(seq)
              p.barrier()

        if PHASES >= 4:
            attn_b_phase()

        if PHASES >= 5:
            ffn_sublayer(last=True)

        fin = [(s_, p.last(s_)) for s_ in p.cnt if s_ != "Y"]
        p.cnt["Y"] += 1
        p.ops["sync"].append((fin, lambda e: e.nop(), "Y", 1, False))

        p.flush(es.enter_context(nc.Block()))
    return nc


_NC_CACHE = {}


def _get_nc(taps=()):
    key = tuple(taps)
    if key not in _NC_CACHE:
        _NC_CACHE[key] = build_nc(key)
    return _NC_CACHE[key]


def kernel(L0_attn_a_x, L0_attn_a_w_q, L0_attn_a_w_k, L0_attn_a_w_v, L0_attn_a_w_o,
           L0_attn_b_Wqkv, L0_attn_b_bqkv, L0_attn_b_Wproj, L0_attn_b_bproj,
           L0_attn_b_gamma, L0_ffn_G1_R, L0_ffn_G2_R, L0_ffn_G1_I, L0_ffn_G2_I,
           L0_ffn_bias_real, L0_ffn_bias_imag, _collect=None):
    f32 = np.float32
    x = np.ascontiguousarray(np.asarray(L0_attn_a_x, f32))
    G1R = np.asarray(L0_ffn_G1_R, f32); G2R = np.asarray(L0_ffn_G2_R, f32)
    G1I = np.asarray(L0_ffn_G1_I, f32); G2I = np.asarray(L0_ffn_G2_I, f32)

    def expand(G1, G2):
        t1 = np.transpose(G1[0], (1, 2, 0))
        Ae = np.broadcast_to(t1[:, :, None, :, None],
                             (4, 32, 16, 32, 16)).reshape(4, 512, 512)
        t2 = np.transpose(G2[:, :, 0, :], (0, 2, 1))
        Be = np.broadcast_to(t2[:, None, :, None, :],
                             (4, 32, 16, 32, 16)).reshape(4, 512, 512)
        return np.ascontiguousarray(Ae), np.ascontiguousarray(Be)

    AeR, BeR = expand(G1R, G2R)
    AeI, BeI = expand(G1I, G2I)
    shared = {
        "wq": np.ascontiguousarray(np.asarray(L0_attn_a_w_q, f32)),
        "wk": np.ascontiguousarray(np.asarray(L0_attn_a_w_k, f32)),
        "wv": np.ascontiguousarray(np.asarray(L0_attn_a_w_v, f32)),
        "wo": np.ascontiguousarray(np.asarray(L0_attn_a_w_o, f32)),
        "wqkv": np.ascontiguousarray(np.asarray(L0_attn_b_Wqkv, f32)),
        "bqkv": np.asarray(L0_attn_b_bqkv, f32).reshape(1, 3072),
        "wproj": np.ascontiguousarray(np.asarray(L0_attn_b_Wproj, f32)),
        "bproj": np.asarray(L0_attn_b_bproj, f32).reshape(1, 1024),
        "gam": np.asarray(L0_attn_b_gamma, f32).reshape(1, 1),
        "aexp": AeR, "bexp": BeR, "aexpi": AeI, "bexpi": BeI,
        "bffn": np.concatenate([np.asarray(L0_ffn_bias_real, f32),
                                np.asarray(L0_ffn_bias_imag, f32)]).reshape(1, 1024),
    }
    in_maps = []
    for c in range(NCORE):
        m = dict(shared)
        m["x"] = x[c * SEQ:(c + 1) * SEQ]
        in_maps.append(m)
    nc = _get_nc(tuple(DEBUG_TAPS))
    res = run_bass_kernel_spmd(nc, in_maps, core_ids=list(range(NCORE)))
    if _collect is not None:
        _collect.extend(res.results)
    out = np.concatenate([r["out"] for r in res.results], axis=0)
    return out.reshape(16, 2048, 1024)
